# version 14
# baseline (speedup 1.0000x reference)
import contextlib
import numpy as np

import concourse.bass as bass
import concourse.mybir as mybir
from concourse.bass_utils import run_bass_kernel_spmd

F32 = mybir.dt.float32
BF16 = mybir.dt.bfloat16
AF = mybir.ActivationFunctionType
ALU = mybir.AluOpType
AX = mybir.AxisListType

D = 1024
DK = D // 128
SEQ = 8192
OWN = 4096
TT = 512
E = 8
HF = 4096
NH = 16
DH = 64
NEG_BIG = -1e30
LN_EPS = 1e-5
RMS_EPS = 1e-6


class Sem:
    def __init__(self, h):
        self.h = h
        self.n = 0


class DmaQ:

    def __init__(self, kb, eng, name, K=4):
        self.kb = kb
        self.eng = eng
        self.lanes = [kb.sem(f"{name}{i}") for i in range(K)]
        self.K = K
        self.i = 0

    def dma(self, fn):
        lane = self.lanes[self.i % self.K]
        self.i += 1
        if lane.n > 0:
            self.kb.wait(self.eng, lane, lane.n)
        n = self.kb.emit(self.eng, fn, sig=lane, dma=True)
        return (lane, n)

    def token(self):
        return [(l, l.n) for l in self.lanes if l.n > 0]


class KB:

    ENGS = ("t", "v", "s", "g", "y")

    def __init__(self):
        self.nc = bass.Bass("TRN2", target_bir_lowering=False)
        self.es = contextlib.ExitStack()
        self.S = {e: [] for e in self.ENGS}

    def sem(self, name):
        return Sem(self.es.enter_context(self.nc.semaphore(name)))

    def sbuf(self, name, shp, dt):
        return self.es.enter_context(self.nc.sbuf_tensor(name, shp, dt))

    def psum(self, name, shp, dt=F32):
        return self.es.enter_context(self.nc.psum_tensor(name, shp, dt))

    def emit(self, eng, fn, sig=None, dma=False):
        if sig is None:
            self.S[eng].append(fn)
            return None
        k = 16 if dma else 1
        sig.n += k
        self.S[eng].append(lambda e, fn=fn, s=sig, k=k: fn(e).then_inc(s.h, k))
        return sig.n

    def wait(self, eng, sem, val):
        if val <= 0:
            return
        key = (eng, id(sem))
        if not hasattr(self, "_lw"):
            self._lw = {}
        if self._lw.get(key, -1) >= val:
            return
        self._lw[key] = val
        self.S[eng].append(lambda e, s=sem, v=val: e.wait_ge(s.h, v))

    def waittok(self, eng, tok):
        if tok is None:
            return
        if isinstance(tok, list):
            for t in tok:
                self.waittok(eng, t)
            return
        sem, val = tok
        self.wait(eng, sem, val)

    def barrier(self):
        for q in getattr(self, "dmaqs", ()):
            self.waittok(q.eng, q.token())
        if not hasattr(self, "_bar"):
            self._bar = {e: self.sem(f"bar_{e}") for e in self.ENGS}
        for e in self.ENGS:
            s = self._bar[e]
            s.n += 1
            self.S[e].append(lambda en, s=s: en.nop().then_inc(s.h, 1))
        for e in self.ENGS:
            for o in self.ENGS:
                if o != e:
                    self.wait(e, self._bar[o], self._bar[o].n)

    def finalize(self):
        with self.nc.Block() as block:
            @block.tensor
            def _(t):
                for f in self.S["t"]:
                    f(t)

            @block.vector
            def _(v):
                for f in self.S["v"]:
                    f(v)

            @block.scalar
            def _(s):
                for f in self.S["s"]:
                    f(s)

            @block.gpsimd
            def _(g):
                for f in self.S["g"]:
                    f(g)

            @block.sync
            def _(y):
                for f in self.S["y"]:
                    f(y)



def _tile_w(w):
    din, dout = w.shape
    t = w.reshape(din // 128, 128, dout // 128, 128)
    return np.ascontiguousarray(t.transpose(1, 0, 2, 3))


def _tile_w_om(w):
    din, dout = w.shape
    t = w.reshape(din // 128, 128, dout // 128, 128)
    return np.ascontiguousarray(t.transpose(1, 2, 0, 3))


def host_prepare(inputs):
    HC = HF // 128
    x = np.asarray(inputs["L0_attn_x"], dtype=np.float32)
    wq = np.asarray(inputs["L0_attn_wq"], np.float32).reshape(D, NH * DH)
    wk = np.asarray(inputs["L0_attn_wk"], np.float32).reshape(D, NH * DH)
    wv = np.asarray(inputs["L0_attn_wv"], np.float32).reshape(D, NH * DH)
    w_out = np.asarray(inputs["L0_attn_w_out"], np.float32).reshape(NH * DH, D)
    norm_scale = np.asarray(inputs["L0_attn_norm_scale"], np.float32)
    b_out = np.asarray(inputs["L0_attn_b_out"], np.float32)
    rw1 = np.asarray(inputs["L0_ffn_rw1"], np.float32)
    rb1 = np.asarray(inputs["L0_ffn_rb1"], np.float32)
    rw2 = np.asarray(inputs["L0_ffn_rw2"], np.float32)
    rb2 = np.asarray(inputs["L0_ffn_rb2"], np.float32)
    we1 = np.asarray(inputs["L0_ffn_we1"], np.float32)
    be1 = np.asarray(inputs["L0_ffn_be1"], np.float32)
    we2 = np.asarray(inputs["L0_ffn_we2"], np.float32)
    be2 = np.asarray(inputs["L0_ffn_be2"], np.float32)

    rb2_col = np.zeros((128, 1), np.float32)
    rb2_col[:E, 0] = rb2
    be2_row = np.zeros((8, D), np.float32)
    be2_row[:E] = be2

    shared = dict(
        we1_t=np.stack([_tile_w_om(we1[e]).reshape(128, -1) for e in range(E)]),
        we2_t=np.stack([_tile_w(we2[e]).reshape(128, -1) for e in range(E)]),
        rw1_t=_tile_w(rw1).reshape(128, -1),
        wq_t=_tile_w(wq).reshape(128, -1),
        wk_t=_tile_w(wk).reshape(128, -1),
        wv_t=_tile_w(wv).reshape(128, -1),
        wo_t=_tile_w(w_out).reshape(128, -1),
        be1_t=np.ascontiguousarray(be1.reshape(E, HC, 128).transpose(0, 2, 1)),
        be2_row=be2_row,
        rb1_t=np.ascontiguousarray(rb1.reshape(DK, 128).T),
        scale_t=np.ascontiguousarray(norm_scale.reshape(DK, 128).T),
        bout_t=np.ascontiguousarray(b_out.reshape(DK, 128).T),
        rb2_col=rb2_col,
        rw2_t=np.ascontiguousarray(
            rw2.reshape(DK, 128, E).transpose(1, 0, 2)).reshape(128, DK * E),
        ident=np.eye(128, dtype=np.float32),
        esel=np.ascontiguousarray(
            np.eye(E, dtype=np.float32)[:, :, None].repeat(128, 2).reshape(
                E, E * 128)),
    )

    in_maps = []
    for c in range(2 * x.shape[0]):
        b, half = c // 2, c % 2
        off = half * OWN
        xb = x[b]
        xp = np.concatenate([xb[off:off + OWN], xb[OWN - off:2 * OWN - off]],
                            axis=0)
        m = dict(shared)
        m["x_fm"] = np.ascontiguousarray(xp.T)
        in_maps.append(m)
    return in_maps


def host_assemble(results):
    out = np.empty((len(results) // 2, SEQ, D), np.float32)
    for c in range(len(results)):
        b, half = c // 2, c % 2
        off = half * OWN
        out[b, off:off + OWN] = results[c]["out_fm"].T
    return out



def declare_io(kb):
    nc = kb.nc
    HC = HF // 128
    HD = NH * DH
    ap = lambda name, shp, dt=F32, kind="ExternalInput": nc.dram_tensor(
        name, shp, dt, kind=kind).ap()
    io = {}
    io["x_fm"] = ap("x_fm", [D, SEQ])
    io["we1_t"] = ap("we1_t", [E, 128, HC * DK * 128])
    io["we2_t"] = ap("we2_t", [E, 128, HC * DK * 128])
    io["rw1_t"] = ap("rw1_t", [128, DK * DK * 128])
    io["wq_t"] = ap("wq_t", [128, DK * (HD // 128) * 128])
    io["wk_t"] = ap("wk_t", [128, DK * (HD // 128) * 128])
    io["wv_t"] = ap("wv_t", [128, DK * (HD // 128) * 128])
    io["wo_t"] = ap("wo_t", [128, (HD // 128) * DK * 128])
    io["be1_t"] = ap("be1_t", [E, 128, HC])
    io["be2_row"] = ap("be2_row", [8, D])
    io["rb1_t"] = ap("rb1_t", [128, DK])
    io["scale_t"] = ap("scale_t", [128, DK])
    io["bout_t"] = ap("bout_t", [128, DK])
    io["rb2_col"] = ap("rb2_col", [128, 1])
    io["rw2_t"] = ap("rw2_t", [128, DK * E])
    io["ident"] = ap("ident", [128, 128])
    io["esel"] = ap("esel", [E, E * 128])
    io["out_fm"] = ap("out_fm", [D, OWN], kind="ExternalOutput")
    iap = lambda name, shp, dt: nc.dram_tensor(name, shp, dt, kind="Internal").ap()
    io["we1_b"] = iap("we1_b", [E, 128, HC * DK * 128], BF16)
    io["we2_b"] = iap("we2_b", [E, 128, HC * DK * 128], BF16)
    io["rw1_hi"] = iap("rw1_hi", [128, DK * DK * 128], BF16)
    io["rw1_lo"] = iap("rw1_lo", [128, DK * DK * 128], BF16)
    io["y1_d"] = iap("y1_d", [D, SEQ], F32)
    io["y2_d"] = iap("y2_d", [D, OWN], F32)
    io["kf_d"] = iap("kf_d", [HD, SEQ], BF16)
    io["vf_d"] = iap("vf_d", [HD, SEQ], BF16)
    io["o_d"] = iap("o_d", [HD, OWN], BF16)
    return io


def declare_common(kb, io):
    c = {}
    c["onesrow"] = kb.sbuf("onesrow", [1, 128], F32)
    c["onescol"] = kb.sbuf("onescol", [128, 1], F32)
    c["identf"] = kb.sbuf("identf", [128, 128], F32)
    c["identb"] = kb.sbuf("identb", [128, 128], BF16)
    c["rb2c"] = kb.sbuf("rb2c", [128, 1], F32)
    c["rb1c"] = kb.sbuf("rb1c", [128, DK], F32)
    c["scalec"] = kb.sbuf("scalec", [128, DK], F32)
    c["boutc"] = kb.sbuf("boutc", [128, DK], F32)
    c["be1c"] = kb.sbuf("be1c", [128, E * (HF // 128)], F32)
    c["be2r"] = kb.sbuf("be2r", [8, D], F32)
    c["rw2c"] = kb.sbuf("rw2c", [128, DK * E], F32)
    c["esel"] = kb.sbuf("eselc", [E, E * 128], F32)
    c["cst"] = kb.sbuf("cst", [128, 4], F32)
    c["rw2h"] = kb.sbuf("rw2h", [128, DK * E], BF16)
    c["rw2l"] = kb.sbuf("rw2l", [128, DK * E], BF16)
    c["ps"] = [kb.psum(f"ps{i}", [128, 512]) for i in range(7)]
    c["ptb"] = kb.psum("ptb", [128, 1024], BF16)
    for s in ("sm", "sv", "ss", "sg"):
        c[s] = kb.sem(s)
    c["dg"] = DmaQ(kb, "g", "dg", K=4)
    c["dy"] = DmaQ(kb, "y", "dy", K=4)
    c["dw"] = DmaQ(kb, "y", "dw", K=6)
    kb.dmaqs = (c["dg"], c["dy"], c["dw"])
    return c


def build_prepass(kb, io, c):
    sv, sg, dg, dy = c["sv"], c["sg"], c["dg"], c["dy"]
    for dst, src in (
        (c["identf"], io["ident"]), (c["rb2c"], io["rb2_col"]),
        (c["rb1c"], io["rb1_t"]), (c["scalec"], io["scale_t"]),
        (c["boutc"], io["bout_t"]), (c["be2r"], io["be2_row"]),
        (c["rw2c"], io["rw2_t"]), (c["esel"], io["esel"]),
    ):
        dy.dma(lambda e, d=dst, s=src: e.dma_start(out=d[:], in_=s[:]))
    dy.dma(lambda e: e.dma_start(
        out=c["be1c"][:].rearrange("p (e h) -> p e h", e=E),
        in_=io["be1_t"].rearrange("e p h -> p e h")))
    tok_smalls = dy.token()
    kb.emit("g", lambda e: e.memset(c["onesrow"][:], 1.0), sig=sg)
    kb.emit("g", lambda e: e.memset(c["onescol"][:], 1.0), sig=sg)
    kb.emit("g", lambda e: e.memset(c["cst"][:, 0:1], LN_EPS), sig=sg)
    kb.emit("g", lambda e: e.memset(c["cst"][:, 1:2], RMS_EPS), sig=sg)
    c["sg_ones"] = sg.n
    kb.waittok("v", tok_smalls)
    kb.emit("v", lambda e: e.tensor_copy(c["identb"][:], c["identf"][:]), sig=sv)
    kb.emit("v", lambda e: e.tensor_copy(c["rw2h"][:], c["rw2c"][:]), sig=sv)
    kb.wait("v", sv, sv.n)
    kb.emit("v", lambda e: e.tensor_tensor(
        c["rw2l"][:], c["rw2c"][:], c["rw2h"][:], ALU.subtract), sig=sv)
    c["n_small_v"] = sv.n
    c["tok_smalls"] = tok_smalls

    cw = DK * DK * 128
    with kb.nc.sbuf_tensor("cb_f", [128, cw], F32) as cb_f, \
         kb.nc.sbuf_tensor("cb_h", [128, cw], BF16) as cb_h, \
         kb.nc.sbuf_tensor("cb_l", [128, cw], BF16) as cb_l:
        t_in = dg.dma(lambda e: e.dma_start(out=cb_f[:], in_=io["rw1_t"][:]))
        kb.waittok("v", t_in)
        kb.emit("v", lambda e: e.tensor_copy(cb_h[:], cb_f[:]), sig=sv)
        kb.wait("v", sv, sv.n)
        kb.emit("v", lambda e: e.tensor_tensor(
            cb_l[:], cb_f[:], cb_h[:], ALU.subtract), sig=sv)
        v_done = sv.n
        kb.wait("g", sv, v_done)
        dg.dma(lambda e: e.dma_start(out=io["rw1_hi"][:], in_=cb_h[:]))
        dg.dma(lambda e: e.dma_start(out=io["rw1_lo"][:], in_=cb_l[:]))
    c["tok_rw1"] = dg.token()


_FFN_N = [0]


def build_ffn(kb, io, c, src, dst, ntiles, first=False):
    sm, sv, ss, sg, dg, dy = c["sm"], c["sv"], c["ss"], c["sg"], c["dg"], c["dy"]
    ps, onesrow, onescol = c["ps"], c["onesrow"], c["onescol"]
    HC = HF // 128
    NC4 = TT // 128
    GHC = 2
    NS = 4
    NGRP = HC // GHC
    GW = GHC * DK * 128

    _FFN_N[0] += 1
    tag = f"F{_FFN_N[0]}"
    with contextlib.ExitStack() as stk:
        sb = lambda n, s, d: stk.enter_context(kb.nc.sbuf_tensor(tag + n, s, d))
        xt = sb("f_xt", [128, DK * TT], F32)
        zh2 = sb("f_zh", [128, 2 * DK * TT], BF16)
        zl = sb("f_zl", [128, DK * TT], BF16)
        zf = sb("f_zf", [128, TT], F32)
        acc = sb("f_acc", [128, DK * TT], F32)
        h_sb = sb("f_h", [128, HC * TT], BF16)
        w1b = sb("f_w1b", [128, NS * GW], BF16)
        w2b = sb("f_w2b", [128, NS * GW], BF16)
        r1f = sb("f_r1f", [128, 2 * TT], F32)
        sq = r1f
        r1h = sb("f_r1h", [128, DK * TT], BF16)
        r1l = sb("f_r1l", [128, DK * TT], BF16)
        wbc = sb("f_wbc", [128, 2 * TT], F32)
        eot = sb("f_eot", [128, 2 * TT], F32)
        rows = sb("f_rows", [1, 9 * TT], F32)
        lg8 = sb("f_lg8", [128, E], F32)
        l28 = sb("f_l28", [128, E], F32)
        l38 = sb("f_l38", [128, E], F32)
        m1x8 = sb("f_m1x8", [128, E], F32)
        m2x8 = sb("f_m2x8", [128, E], F32)
        lgs = sb("f_lgs", [8, TT], F32)
        m12 = sb("f_m12", [128, 8 * NC4], F32)
        tp8 = sb("f_tp8", [128, 4 * E], F32)
        wtm = sb("f_wtm", [128, NC4 * E], F32)
        wfm = sb("f_wfm", [8, TT], F32)
        if first:
            qw = HC * DK * 128 // 16
            cvt = sb("f_cvt", [128, 2 * qw], BF16)
            conv_state = {"out_done": [None, None]}
            c["tok_exp"] = [None] * E

            def emit_convert(e_i):
                dgq = c["dg"]
                for mat_in, mat_out in ((io["we1_t"], io["we1_b"]),
                                        (io["we2_t"], io["we2_b"])):
                    for q in range(16):
                        h2 = q % 2
                        kb.waittok("g", conv_state["out_done"][h2])
                        t_in = dgq.dma(lambda e, m=mat_in, ei=e_i, q=q, h2=h2:
                                       e.dma_start(
                            out=cvt[:, h2 * qw:(h2 + 1) * qw],
                            in_=m[ei, :, q * qw:(q + 1) * qw]))
                        kb.waittok("g", t_in)
                        conv_state["out_done"][h2] = dgq.dma(
                            lambda e, m=mat_out, ei=e_i, q=q, h2=h2:
                            e.dma_start(
                                out=m[ei, :, q * qw:(q + 1) * qw],
                                in_=cvt[:, h2 * qw:(h2 + 1) * qw]))
                c["tok_exp"][e_i] = c["dg"].token()

        kb.waittok("g", c["tok_rw1"])

        row_sl = lambda i: rows[0:1, i * TT:(i + 1) * TT]
        R_MEAN, R_MS, R_SQM, R_VAR, R_SD, R_RSTD, R_B, R_A2, R_B2 = range(9)

        src_r = src.rearrange("(dk p) s -> p dk s", p=128)
        dst_r = dst.rearrange("(dk p) s -> p dk s", p=128)

        st_ = {
            "acc_dma": None,
            "v_accinit": 0,
            "s_sq": 0,
            "x_tok": None,
            "v_z": 0,
            "sm_bnd": 0,
            "g_acc": [0] * DK,
            "g_eot": [0, 0],
            "w1_rd": [0] * NS,
            "w2_rd": [0] * NS,
            "eo_rd": {},
            "v_wbc": 0,
            "v_wfm": 0,
            "rw1_tok": None,
            "sm_we_last": 0,
        }

        def make_prep(ti):
            par = ti % 2
            zh = zh2[:, par * DK * TT:(par + 1) * DK * TT]
            tasks = []
            tsl = slice(ti * TT, (ti + 1) * TT)

            def t_xload():
                kb.wait("y", sv, st_["v_accinit"])
                kb.wait("y", ss, st_["s_sq"])
                st_["x_tok"] = dy.dma(lambda e, t=tsl: e.dma_start(
                    out=xt[:].rearrange("p (dk t) -> p dk t", dk=DK),
                    in_=src_r[:, :, t]))
            tasks.append(t_xload)

            def t_sum():
                kb.waittok("t", st_["x_tok"])
                kb.wait("t", sv, st_["v_accinit"])
                for dk in range(DK):
                    kb.emit("t", lambda e, dk=dk, last=(dk == DK - 1):
                            e.matmul(
                        ps[1][0:1, :TT], onescol[:, 0:1],
                        xt[:, dk * TT:(dk + 1) * TT],
                        start=(dk == 0), stop=last, skip_group_check=True),
                        sig=(sm if dk == DK - 1 else None))
                sm_sum = sm.n
                kb.wait("v", sm, sm_sum)
                kb.wait("v", sm, st_["sm_bnd"])
                kb.emit("v", lambda e: e.tensor_scalar(
                    row_sl(R_MEAN), ps[1][0:1, :TT], 1.0 / D, None, ALU.mult),
                    sig=sv)
            tasks.append(t_sum)

            def t_sq(dk):
                kb.waittok("s", st_["x_tok"])
                if dk >= 2:
                    kb.wait("s", sm, st_["_sm_sq2"])
                kb.emit("s", lambda e, dk=dk: e.activation(
                    sq[:, (dk % 2) * TT:(dk % 2 + 1) * TT],
                    xt[:, dk * TT:(dk + 1) * TT], AF.Square), sig=ss)
                st_["s_sq"] = ss.n
                kb.wait("t", ss, ss.n)
                if dk == 0:
                    kb.wait("t", sv, sv.n)
                kb.emit("t", lambda e, dk=dk, last=(dk == DK - 1): e.matmul(
                    ps[1][0:1, :TT], onescol[:, 0:1],
                    sq[:, (dk % 2) * TT:(dk % 2 + 1) * TT],
                    start=(dk == 0), stop=last, skip_group_check=True), sig=sm)
                st_["_sm_sq2"] = sm.n
            for dk in range(DK):
                tasks.append(lambda dk=dk: t_sq(dk))

            def t_chain():
                kb.wait("v", sm, st_["_sm_sq2"])
                kb.emit("v", lambda e: e.tensor_scalar(
                    row_sl(R_MS), ps[1][0:1, :TT], 1.0 / D, None, ALU.mult),
                    sig=sv)
                v_ms = sv.n
                kb.wait("s", sv, v_ms)
                kb.emit("s", lambda e: e.activation(
                    row_sl(R_SQM), row_sl(R_MEAN), AF.Square), sig=ss)
                s_sqm = ss.n
                kb.wait("v", ss, s_sqm)
                kb.emit("v", lambda e: e.tensor_tensor(
                    row_sl(R_VAR), row_sl(R_MS), row_sl(R_SQM), ALU.subtract),
                    sig=sv)
                v_var = sv.n
                kb.wait("s", sv, v_var)
                kb.wait("s", c["sg"], c["sg_ones"])
                kb.emit("s", lambda e: e.activation(
                    row_sl(R_SD), row_sl(R_VAR), AF.Sqrt,
                    bias=c["cst"][0:1, 0:1]), sig=ss)
                s_sd = ss.n
                kb.wait("v", ss, s_sd)
                kb.emit("v", lambda e: e.reciprocal(
                    row_sl(R_RSTD), row_sl(R_SD)), sig=sv)
                kb.wait("v", sv, sv.n)
                kb.emit("v", lambda e: e.tensor_tensor(
                    row_sl(R_B), row_sl(R_MEAN), row_sl(R_RSTD), ALU.mult),
                    sig=sv)
                kb.emit("v", lambda e: e.tensor_scalar(
                    row_sl(R_A2), row_sl(R_RSTD), 0.5, 1.0, ALU.mult, ALU.add),
                    sig=sv)
                kb.wait("v", sv, sv.n)
                kb.emit("v", lambda e: e.tensor_scalar(
                    row_sl(R_B2), row_sl(R_B), 0.5, None, ALU.mult), sig=sv)
                st_["_v_rows"] = sv.n
            tasks.append(t_chain)

            def t_bc():
                kb.wait("t", sv, st_["_v_rows"])
                kb.emit("t", lambda e: e.matmul(
                    ps[1][:, :TT], onesrow[:, :], row_sl(R_RSTD),
                    start=True, stop=True, skip_group_check=True), sig=sm)
                kb.emit("t", lambda e: e.matmul(
                    ps[3][:, :TT], onesrow[:, :], row_sl(R_B),
                    start=True, stop=True, skip_group_check=True), sig=sm)
                st_["_sm_bc"] = sm.n
            tasks.append(t_bc)

            def t_z(dk):
                dsl = slice(dk * TT, (dk + 1) * TT)
                kb.wait("v", sm, st_["_sm_bc"])
                kb.wait("v", sm, st_["sm_bnd"])
                kb.wait("v", sv, sv.n)
                kb.emit("v", lambda e, dsl=dsl: e.tensor_tensor(
                    zf[:], xt[:, dsl], ps[1][:, :TT], ALU.mult), sig=sv)
                kb.wait("v", sv, sv.n)
                kb.emit("v", lambda e, dsl=dsl: e.tensor_tensor(
                    zh[:, dsl], zf[:], ps[3][:, :TT], ALU.subtract), sig=sv)
                kb.emit("v", lambda e: e.tensor_tensor(
                    sq[:, TT:2 * TT], zf[:], ps[3][:, :TT], ALU.subtract),
                    sig=sv)
                kb.wait("v", sv, sv.n)
                kb.emit("v", lambda e, dsl=dsl: e.tensor_tensor(
                    zl[:, dsl], sq[:, TT:2 * TT], zh[:, dsl], ALU.subtract),
                    sig=sv)
                st_["v_z"] = sv.n
            for dk in range(DK):
                tasks.append(lambda dk=dk: t_z(dk))
            return tasks

        def boundary(ti):
            par = ti % 2
            zh = zh2[:, par * DK * TT:(par + 1) * DK * TT]
            kb.wait("g", sm, max(st_["w1_rd"] + st_["w2_rd"]))
            st_["rw1_tok"] = [
                dg.dma(lambda e: e.dma_start(out=w1b[:], in_=io["rw1_hi"][:])),
                dg.dma(lambda e: e.dma_start(out=w2b[:], in_=io["rw1_lo"][:]))]
            kb.wait("t", sv, st_["v_z"])
            kb.wait("t", sv, st_["v_wbc"])
            kb.emit("t", lambda e: e.matmul(
                ps[2][:, :TT], onesrow[:, :], row_sl(R_A2),
                start=True, stop=True, skip_group_check=True), sig=sm)
            kb.emit("t", lambda e: e.matmul(
                ps[3][:, :TT], onesrow[:, :], row_sl(R_B2),
                start=True, stop=True, skip_group_check=True), sig=sm)
            sm_bc2 = sm.n
            kb.wait("v", sm, sm_bc2)
            kb.waittok("v", st_["acc_dma"])
            for dk in range(DK):
                dsl = slice(dk * TT, (dk + 1) * TT)
                kb.wait("v", sv, sv.n)
                kb.emit("v", lambda e, dsl=dsl: e.tensor_tensor(
                    zf[:], xt[:, dsl], ps[2][:, :TT], ALU.mult), sig=sv)
                kb.wait("v", sv, sv.n)
                kb.wait("v", c["sg"], st_["g_acc"][dk])
                kb.emit("v", lambda e, dsl=dsl: e.tensor_tensor(
                    acc[:, dsl], zf[:], ps[3][:, :TT], ALU.subtract), sig=sv)
            st_["v_accinit"] = sv.n

            kb.waittok("t", st_["rw1_tok"])
            terms = ((w1b, zh), (w1b, zl), (w2b, zh))
            s_r1_start = ss.n
            for dm in range(DK):
                pr = ps[4 + dm % 2]
                if dm >= 2:
                    kb.wait("t", ss, s_r1_start + dm - 1)
                nmm = 0
                for (wsrc, zsrc) in terms:
                    for dk in range(DK):
                        nmm += 1
                        kb.emit("t", lambda e, w=wsrc, z=zsrc, dm=dm, dk=dk,
                                pr=pr, first=(nmm == 1),
                                last=(nmm == 3 * DK): e.matmul(
                            pr[:, :TT],
                            w[:, (dk * DK + dm) * 128:(dk * DK + dm + 1) * 128],
                            z[:, dk * TT:(dk + 1) * TT],
                            start=first, stop=last, skip_group_check=True),
                            sig=(sm if nmm == 3 * DK else None))
                kb.wait("s", sm, sm.n)
                kb.wait("s", sv, sv.n)
                kb.emit("s", lambda e, dm=dm, pr=pr: e.activation(
                    r1f[:, (dm % 2) * TT:(dm % 2 + 1) * TT], pr[:, :TT],
                    AF.Silu, bias=c["rb1c"][:, dm:dm + 1]), sig=ss)
                s_r1dm = ss.n
                kb.wait("v", ss, s_r1dm)
                kb.emit("v", lambda e, dm=dm: e.tensor_copy(
                    r1h[:, dm * TT:(dm + 1) * TT],
                    r1f[:, (dm % 2) * TT:(dm % 2 + 1) * TT]), sig=sv)
                kb.wait("v", sv, sv.n)
                kb.emit("v", lambda e, dm=dm: e.tensor_tensor(
                    r1l[:, dm * TT:(dm + 1) * TT],
                    r1f[:, (dm % 2) * TT:(dm % 2 + 1) * TT],
                    r1h[:, dm * TT:(dm + 1) * TT], ALU.subtract), sig=sv)
            v_r1 = sv.n
            st_["w1_rd"] = [sm.n] * NS
            st_["w2_rd"] = [sm.n] * NS

            kb.wait("t", sv, v_r1)
            kb.wait("t", sv, c["n_small_v"])
            lterm = ((c["rw2h"], r1h), (c["rw2h"], r1l), (c["rw2l"], r1h))
            nmm = 0
            for (wsrc, rsrc) in lterm:
                for dk in range(DK):
                    nmm += 1
                    kb.emit("t", lambda e, w=wsrc, r=rsrc, dk=dk,
                            first=(nmm == 1), last=(nmm == 3 * DK): e.matmul(
                        ps[4][0:8, :TT], w[:, dk * E:(dk + 1) * E],
                        r[:, dk * TT:(dk + 1) * TT],
                        start=first, stop=last, skip_group_check=True),
                        sig=(sm if nmm == 3 * DK else None))
            sm_lg = sm.n
            kb.wait("s", sm, sm_lg)
            kb.emit("s", lambda e: e.activation(
                lgs[:, :TT], ps[4][0:8, :TT], AF.Identity,
                bias=c["rb2c"][0:8, 0:1]), sig=ss)
            s_lgs = ss.n
            kb.wait("t", ss, s_lgs)
            for cc in range(NC4):
                kb.emit("t", lambda e, cc=cc: e.transpose(
                    ps[5][:, cc * 8:(cc + 1) * 8],
                    lgs[0:8, cc * 128:(cc + 1) * 128], c["identf"][0:8, 0:8]),
                    sig=(sm if cc == NC4 - 1 else None))
            sm_lgT = sm.n
            kb.wait("v", sm, sm_lgT)

            for cc in range(NC4):
                L, L2, L3 = lg8[:, :], l28[:, :], l38[:, :]
                kb.wait("v", sv, sv.n)
                kb.emit("v", lambda e, cc=cc: e.tensor_copy(
                    lg8[:, :], ps[5][:, cc * E:(cc + 1) * E]), sig=sv)
                m1 = m12[:, cc:cc + 1]
                m2 = m12[:, NC4 + cc:NC4 + cc + 1]
                m2n = m12[:, 2 * NC4 + cc:2 * NC4 + cc + 1]
                w1 = m12[:, 3 * NC4 + cc:3 * NC4 + cc + 1]
                w2 = m12[:, 4 * NC4 + cc:4 * NC4 + cc + 1]
                d12 = m12[:, 5 * NC4 + cc:5 * NC4 + cc + 1]
                w1r = m12[:, 6 * NC4 + cc:6 * NC4 + cc + 1]
                kb.wait("v", sv, sv.n)
                kb.emit("v", lambda e, L=L, m1=m1: e.reduce_max(
                    m1, L, axis=AX.X), sig=sv)
                kb.wait("v", sv, sv.n)
                kb.emit("v", lambda e, L=L, m1=m1: e.tensor_scalar(
                    m1x8[:, :], L, 0.0, m1, ALU.mult, ALU.add), sig=sv)
                kb.wait("v", sv, sv.n)
                kb.emit("v", lambda e, L=L, L2=L2: e.match_replace(
                    L2, m1x8[:, :], L, NEG_BIG), sig=sv)
                kb.wait("v", sv, sv.n)
                kb.emit("v", lambda e, L2=L2, m2=m2: e.reduce_max(
                    m2, L2, axis=AX.X), sig=sv)
                kb.emit("v", lambda e, L2=L2, m2n=m2n: e.tensor_reduce(
                    m2n, L2, axis=AX.X, op=ALU.max, negate=True), sig=sv)
                kb.wait("v", sv, sv.n)
                kb.emit("v", lambda e, L2=L2, m2=m2: e.tensor_scalar(
                    m2x8[:, :], L2, 0.0, m2, ALU.mult, ALU.add), sig=sv)
                kb.wait("v", sv, sv.n)
                kb.emit("v", lambda e, L2=L2, L3=L3: e.match_replace(
                    L3, m2x8[:, :], L2, NEG_BIG), sig=sv)
                v_l3 = sv.n
                kb.wait("s", sv, v_l3)
                kb.emit("s", lambda e, m1=m1, m2n=m2n, w1r=w1r: e.activation(
                    w1r, m1, AF.Sigmoid, bias=m2n), sig=ss)
                s_w1r = ss.n
                kb.wait("v", ss, s_w1r)
                kb.emit("v", lambda e, w1r=w1r, w1=w1: e.tensor_scalar(
                    w1, w1r, 0.5, None, ALU.mult), sig=sv)
                kb.wait("v", sv, sv.n)
                kb.emit("v", lambda e, w1=w1, w2=w2: e.tensor_scalar(
                    w2, w1, -1.0, 0.5, ALU.mult, ALU.add), sig=sv)
                kb.wait("v", sv, sv.n)
                kb.emit("v", lambda e, w1=w1, w2=w2, d12=d12: e.tensor_tensor(
                    d12, w1, w2, ALU.subtract), sig=sv)
                kb.emit("v", lambda e, L2=L2: e.tensor_scalar(
                    tp8[:, 0:E], L2, NEG_BIG, None, ALU.is_equal), sig=sv)
                kb.emit("v", lambda e, L3=L3: e.tensor_scalar(
                    tp8[:, E:2 * E], L3, NEG_BIG, None, ALU.is_equal), sig=sv)
                kb.wait("v", sv, sv.n)
                kb.emit("v", lambda e, d12=d12: e.tensor_scalar(
                    tp8[:, 2 * E:3 * E], tp8[:, 0:E], d12, None, ALU.mult),
                    sig=sv)
                kb.emit("v", lambda e, w2=w2: e.tensor_scalar(
                    tp8[:, 3 * E:4 * E], tp8[:, E:2 * E], w2, None, ALU.mult),
                    sig=sv)
                kb.wait("v", sv, sv.n)
                kb.emit("v", lambda e, cc=cc: e.tensor_tensor(
                    wtm[:, cc * E:(cc + 1) * E], tp8[:, 2 * E:3 * E],
                    tp8[:, 3 * E:4 * E], ALU.add), sig=sv)
            v_wtm = sv.n
            kb.wait("t", sv, v_wtm)
            for cc in range(NC4):
                kb.emit("t", lambda e, cc=cc: e.transpose(
                    ps[5][0:8, cc * 128:(cc + 1) * 128],
                    wtm[:, cc * E:(cc + 1) * E], c["identf"][:, :]),
                    sig=(sm if cc == NC4 - 1 else None))
            sm_wT = sm.n
            kb.wait("v", sm, sm_wT)
            kb.emit("v", lambda e: e.tensor_copy(
                wfm[0:8, :TT], ps[5][0:8, :TT]), sig=sv)
            st_["v_wfm"] = sv.n
            st_["sm_bnd"] = sm.n

        def expert_loop(ti, pending):
            par = ti % 2
            zh = zh2[:, par * DK * TT:(par + 1) * DK * TT]
            ndrain = 8 * (HC + 2 * HC // 2)
            per = (len(pending) + ndrain - 1) // max(1, ndrain)
            ptr = [0]

            def drain(n=1):
                for _ in range(n):
                    if ptr[0] < len(pending):
                        pending[ptr[0]]()
                        ptr[0] += 1

            for e_i in range(E):
                if first and ti == 0:
                    if e_i == 0:
                        emit_convert(0)
                    if e_i + 1 < E:
                        emit_convert(e_i + 1)
                kb.waittok("y", c["tok_exp"][e_i])
                kb.wait("t", sv, st_["v_wfm"])
                kb.wait("t", sv, st_["v_wbc"])
                kb.wait("t", sv, st_["v_accinit"])
                kb.emit("t", lambda e, ei=e_i: e.matmul(
                    ps[2][:, :TT], c["esel"][:, ei * 128:(ei + 1) * 128],
                    wfm[0:8, :TT],
                    start=True, stop=True, skip_group_check=True), sig=sm)
                sm_wbc = sm.n
                kb.wait("v", sm, sm_wbc)
                kb.wait("v", sv, sv.n)
                kb.emit("v", lambda e, ei=e_i: e.tensor_copy(
                    wbc[:, (ei % 2) * TT:(ei % 2 + 1) * TT], ps[2][:, :TT]),
                    sig=sv)
                st_["v_wbc"] = sv.n
                v_wbc = sv.n

                w1_loads = []
                silu_cnt = []
                dw = c["dw"]
                for hc in range(HC):
                    grp = hc // GHC
                    h2 = grp % NS
                    if hc % GHC == 0:
                        kb.wait("y", sm, st_["w1_rd"][h2])
                        w1_loads.append(dw.dma(
                            lambda e, ei=e_i, grp=grp, h2=h2: e.dma_start(
                                out=w1b[:, h2 * GW:(h2 + 1) * GW],
                                in_=io["we1_b"][ei, :,
                                                grp * GW:(grp + 1) * GW])))
                    ph = ps[4 + hc % 2]
                    kb.waittok("t", w1_loads[grp])
                    if hc < 2:
                        kb.wait("t", sv, st_["eo_rd"].get(id(ph), 0))
                    if hc >= 2:
                        kb.wait("t", ss, silu_cnt[hc - 2])
                    for dk in range(DK):
                        kb.emit("t", lambda e, hc=hc, dk=dk, ph=ph, h2=h2,
                                first=(dk == 0), last=(dk == DK - 1): e.matmul(
                            ph[:, :TT],
                            w1b[:, h2 * GW + ((hc % GHC) * DK + dk) * 128:
                                h2 * GW + ((hc % GHC) * DK + dk + 1) * 128],
                            zh[:, dk * TT:(dk + 1) * TT],
                            start=first, stop=last, skip_group_check=True),
                            sig=(sm if dk == DK - 1 else None))
                    if hc % GHC == GHC - 1:
                        st_["w1_rd"][h2] = sm.n
                    kb.wait("s", sm, sm.n)
                    kb.emit("s", lambda e, hc=hc, ei=e_i, ph=ph: e.activation(
                        h_sb[:, hc * TT:(hc + 1) * TT], ph[:, :TT], AF.Silu,
                        bias=c["be1c"][:, ei * HC + hc:ei * HC + hc + 1]),
                        sig=ss)
                    silu_cnt.append(ss.n)
                    drain(per)
                s_h = ss.n

                for half_i in range(2):
                    dms = list(range(half_i * (DK // 2),
                                     (half_i + 1) * (DK // 2)))
                    pe_map = {dm: ps[(4, 5, 6, 0)[dm % (DK // 2)]]
                              for dm in dms}
                    w2_loads = []
                    kb.wait("t", ss, s_h)
                    for hc in range(HC):
                        grp = hc // GHC
                        h2 = grp % NS
                        if hc % GHC == 0:
                            kb.wait("y", sm, st_["w2_rd"][h2])
                            w2_loads.append(c["dw"].dma(
                                lambda e, ei=e_i, grp=grp, h2=h2: e.dma_start(
                                    out=w2b[:, h2 * GW:(h2 + 1) * GW],
                                    in_=io["we2_b"][ei, :,
                                                    grp * GW:(grp + 1) * GW])))
                        kb.waittok("t", w2_loads[grp])
                        grp_last = (hc % GHC == GHC - 1)
                        if hc == 0:
                            for dm in dms:
                                kb.wait("t", sv,
                                        st_["eo_rd"].get(id(pe_map[dm]), 0))
                        for dm in dms:
                            is_last = (hc == HC - 1) and (e_i != 0)
                            kb.emit("t", lambda e, hc=hc, dm=dm, h2=h2,
                                    pe_=pe_map[dm], first=(hc == 0),
                                    last=is_last: e.matmul(
                                pe_[:, :TT],
                                w2b[:, h2 * GW + ((hc % GHC) * DK + dm) * 128:
                                    h2 * GW + ((hc % GHC) * DK + dm + 1) * 128],
                                h_sb[:, hc * TT:(hc + 1) * TT],
                                start=first, stop=last, skip_group_check=True),
                                sig=(sm if (is_last or grp_last) and
                                     dm == dms[-1] else None))
                        if grp_last:
                            st_["w2_rd"][h2] = sm.n
                        drain(per)
                    if e_i == 0:
                        for dm in dms:
                            kb.emit("t", lambda e, dm=dm, pe_=pe_map[dm]:
                                    e.matmul(
                                pe_[:, :TT],
                                c["be2r"][0:8, dm * 128:(dm + 1) * 128],
                                wfm[0:8, :TT],
                                start=False, stop=True, skip_group_check=True),
                                sig=(sm if dm == dms[-1] else None))
                    sm_eo = sm.n
                    kb.wait("v", sm, sm_eo)
                    kb.wait("v", sv, v_wbc)
                    for dm in dms:
                        kb.wait("v", sv, sv.n)
                        kb.wait("v", sg, st_["g_eot"][dm % 2])
                        kb.emit("v", lambda e, dm=dm, ei=e_i,
                                pe_=pe_map[dm]: e.tensor_tensor(
                            eot[:, (dm % 2) * TT:(dm % 2 + 1) * TT],
                            pe_[:, :TT],
                            wbc[:, (ei % 2) * TT:(ei % 2 + 1) * TT],
                            ALU.mult), sig=sv)
                        v_eot = sv.n
                        st_["eo_rd"][id(pe_map[dm])] = v_eot
                        kb.wait("g", sv, v_eot)
                        kb.wait("g", sg, st_["g_acc"][dm])
                        kb.emit("g", lambda e, dm=dm: e.tensor_tensor(
                            acc[:, dm * TT:(dm + 1) * TT],
                            acc[:, dm * TT:(dm + 1) * TT],
                            eot[:, (dm % 2) * TT:(dm % 2 + 1) * TT], ALU.add),
                            sig=sg)
                        st_["g_acc"][dm] = sg.n
                        st_["g_eot"][dm % 2] = sg.n
            drain(10 ** 6)
            kb.wait("y", sg, max(st_["g_acc"]))
            st_["acc_dma"] = dy.dma(lambda e, t=slice(ti * TT, (ti + 1) * TT):
                                    e.dma_start(
                out=dst_r[:, :, t],
                in_=acc[:].rearrange("p (dk t) -> p dk t", dk=DK)))

        for t in make_prep(0):
            t()
        for ti in range(ntiles):
            boundary(ti)
            pending = make_prep(ti + 1) if ti + 1 < ntiles else []
            expert_loop(ti, pending)
        kb.waittok("y", dy.token())
    return


def build_qkv(kb, io, c, q_pair):
    sm, sv, ss, sg, dy, dg = c["sm"], c["sv"], c["ss"], c["sg"], c["dy"], c["dg"]
    ps, onesrow, onescol = c["ps"], c["onesrow"], c["onescol"]
    HD = NH * DH
    NHP = HD // 128
    T2 = min(512, SEQ)
    NT2 = SEQ // T2
    QSC = float(DH ** -0.5)

    with contextlib.ExitStack() as stk:
        sb = lambda n, s, d: stk.enter_context(kb.nc.sbuf_tensor(n, s, d))
        yt = sb("q_yt", [128, DK * T2], F32)
        sq = sb("q_sq", [128, 2 * T2], F32)
        z2 = sb("q_z2", [128, DK * T2], BF16)
        hb = sb("q_hb", [128, DK * T2], BF16)
        zf = sb("q_zf", [128, T2], F32)
        wqb = sb("q_wq", [128, DK * NHP * 128], BF16)
        wkb = sb("q_wk", [128, DK * NHP * 128], BF16)
        wvb = sb("q_wv", [128, DK * NHP * 128], BF16)
        kst = sb("q_kst", [128, NHP * T2], BF16)
        vst = sb("q_vst", [128, NHP * T2], BF16)
        rows = sb("q_rows", [1, 8 * T2], F32)

        wq_ld = dg.dma(lambda e: e.dma_start(out=wqb[:], in_=io["wq_t"][:]))
        wk_ld = dg.dma(lambda e: e.dma_start(out=wkb[:], in_=io["wk_t"][:]))
        wv_ld = dg.dma(lambda e: e.dma_start(out=wvb[:], in_=io["wv_t"][:]))

        row_sl = lambda i: rows[0:1, i * T2:(i + 1) * T2]
        R_MEAN, R_MS, R_SQM, R_VAR, R_SD, R_RSTD, R_B, R_R2 = range(8)
        src_r = io["y1_d"].rearrange("(dk p) s -> p dk s", p=128)

        st_ = {"y_tok": None, "v_z2": 0, "s_sq": 0, "sm_bnd": 0,
               "_sm_sq2": 0, "_v_rows": 0, "_sm_bc": 0,
               "kv_tok": None, "v_h": 0}

        def make_prep(ti):
            tasks = []
            tsl = slice(ti * T2, (ti + 1) * T2)

            def t_yload():
                kb.wait("y", sv, st_["v_z2"])
                kb.wait("y", ss, st_["s_sq"])
                st_["y_tok"] = dy.dma(lambda e, t=tsl: e.dma_start(
                    out=yt[:].rearrange("p (dk t) -> p dk t", dk=DK),
                    in_=src_r[:, :, t]))
            tasks.append(t_yload)

            def t_sum():
                kb.waittok("t", st_["y_tok"])
                kb.wait("t", sv, st_["v_z2"])
                for dk in range(DK):
                    kb.emit("t", lambda e, dk=dk, last=(dk == DK - 1):
                            e.matmul(
                        ps[1][0:1, :T2], onescol[:, 0:1],
                        yt[:, dk * T2:(dk + 1) * T2],
                        start=(dk == 0), stop=last, skip_group_check=True),
                        sig=(sm if dk == DK - 1 else None))
                sm_sum = sm.n
                kb.wait("v", sm, sm_sum)
                kb.wait("v", sm, st_["sm_bnd"])
                kb.emit("v", lambda e: e.tensor_scalar(
                    row_sl(R_MEAN), ps[1][0:1, :T2], 1.0 / D, None, ALU.mult),
                    sig=sv)
            tasks.append(t_sum)

            def t_sq(dk):
                kb.waittok("s", st_["y_tok"])
                if dk >= 2:
                    kb.wait("s", sm, st_["_sm_sq2"])
                kb.emit("s", lambda e, dk=dk: e.activation(
                    sq[:, (dk % 2) * T2:(dk % 2 + 1) * T2],
                    yt[:, dk * T2:(dk + 1) * T2], AF.Square), sig=ss)
                st_["s_sq"] = ss.n
                kb.wait("t", ss, ss.n)
                if dk == 0:
                    kb.wait("t", sv, sv.n)
                kb.emit("t", lambda e, dk=dk, last=(dk == DK - 1): e.matmul(
                    ps[1][0:1, :T2], onescol[:, 0:1],
                    sq[:, (dk % 2) * T2:(dk % 2 + 1) * T2],
                    start=(dk == 0), stop=last, skip_group_check=True), sig=sm)
                st_["_sm_sq2"] = sm.n
            for dk in range(DK):
                tasks.append(lambda dk=dk: t_sq(dk))

            def t_chain():
                kb.wait("v", sm, st_["_sm_sq2"])
                kb.emit("v", lambda e: e.tensor_scalar(
                    row_sl(R_MS), ps[1][0:1, :T2], 1.0 / D, None, ALU.mult),
                    sig=sv)
                v_ms = sv.n
                kb.wait("s", sv, v_ms)
                kb.emit("s", lambda e: e.activation(
                    row_sl(R_SQM), row_sl(R_MEAN), AF.Square), sig=ss)
                s_sqm = ss.n
                kb.wait("v", ss, s_sqm)
                kb.emit("v", lambda e: e.tensor_tensor(
                    row_sl(R_VAR), row_sl(R_MS), row_sl(R_SQM), ALU.subtract),
                    sig=sv)
                v_var = sv.n
                kb.wait("s", sv, v_var)
                kb.wait("s", c["sg"], c["sg_ones"])
                kb.emit("s", lambda e: e.activation(
                    row_sl(R_SD), row_sl(R_VAR), AF.Sqrt,
                    bias=c["cst"][0:1, 0:1]), sig=ss)
                s_sd = ss.n
                kb.wait("v", ss, s_sd)
                kb.emit("v", lambda e: e.reciprocal(
                    row_sl(R_RSTD), row_sl(R_SD)), sig=sv)
                kb.wait("v", sv, sv.n)
                kb.emit("v", lambda e: e.tensor_tensor(
                    row_sl(R_B), row_sl(R_MEAN), row_sl(R_RSTD), ALU.mult),
                    sig=sv)
                st_["_v_rows"] = sv.n
            tasks.append(t_chain)

            def t_bc():
                kb.wait("t", sv, st_["_v_rows"])
                kb.emit("t", lambda e: e.matmul(
                    ps[2][:, :T2], onesrow[:, :], row_sl(R_RSTD),
                    start=True, stop=True, skip_group_check=True), sig=sm)
                kb.emit("t", lambda e: e.matmul(
                    ps[3][:, :T2], onesrow[:, :], row_sl(R_B),
                    start=True, stop=True, skip_group_check=True), sig=sm)
                st_["_sm_bc"] = sm.n
            tasks.append(t_bc)

            def t_z(dk):
                dsl = slice(dk * T2, (dk + 1) * T2)
                kb.wait("v", sm, st_["_sm_bc"])
                kb.wait("v", sm, st_["sm_bnd"])
                kb.wait("v", sv, sv.n)
                kb.emit("v", lambda e, dsl=dsl: e.tensor_tensor(
                    zf[:], yt[:, dsl], ps[2][:, :T2], ALU.mult), sig=sv)
                kb.wait("v", sv, sv.n)
                kb.emit("v", lambda e, dsl=dsl: e.tensor_tensor(
                    z2[:, dsl], zf[:], ps[3][:, :T2], ALU.subtract), sig=sv)
                st_["v_z2"] = sv.n
            for dk in range(DK):
                tasks.append(lambda dk=dk: t_z(dk))
            return tasks

        def boundary_and_proj(ti, pending):
            per = (len(pending) + 24) // 25
            ptr = [0]

            def drain(n=1):
                for _ in range(n):
                    if ptr[0] < len(pending):
                        pending[ptr[0]]()
                        ptr[0] += 1

            kb.wait("t", sv, st_["v_z2"])
            for dk in range(DK):
                kb.wait("s", sv, st_["v_z2"])
                if dk >= 2:
                    kb.wait("s", sm, sm.n)
                kb.emit("s", lambda e, dk=dk: e.activation(
                    sq[:, (dk % 2) * T2:(dk % 2 + 1) * T2],
                    z2[:, dk * T2:(dk + 1) * T2], AF.Square), sig=ss)
                st_["s_sq"] = ss.n
                kb.wait("t", ss, ss.n)
                if dk == 0:
                    kb.wait("t", sv, sv.n)
                kb.emit("t", lambda e, dk=dk, last=(dk == DK - 1): e.matmul(
                    ps[0][0:1, :T2], onescol[:, 0:1],
                    sq[:, (dk % 2) * T2:(dk % 2 + 1) * T2],
                    start=(dk == 0), stop=last, skip_group_check=True), sig=sm)
            sm_ss2 = sm.n
            kb.wait("v", sm, sm_ss2)
            kb.emit("v", lambda e: e.tensor_scalar(
                row_sl(R_MS), ps[0][0:1, :T2], 1.0 / D, None, ALU.mult), sig=sv)
            v_ms2 = sv.n
            kb.wait("s", sv, v_ms2)
            kb.emit("s", lambda e: e.activation(
                row_sl(R_SD), row_sl(R_MS), AF.Sqrt,
                bias=c["cst"][0:1, 1:2]), sig=ss)
            s_sd2 = ss.n
            kb.wait("v", ss, s_sd2)
            kb.emit("v", lambda e: e.reciprocal(row_sl(R_R2), row_sl(R_SD)),
                    sig=sv)
            v_r2 = sv.n
            kb.wait("t", sv, v_r2)
            kb.emit("t", lambda e: e.matmul(
                ps[0][:, :T2], onesrow[:, :], row_sl(R_R2),
                start=True, stop=True, skip_group_check=True), sig=sm)
            sm_r2bc = sm.n
            kb.wait("v", sm, sm_r2bc)
            for dk in range(DK):
                dsl = slice(dk * T2, (dk + 1) * T2)
                kb.wait("v", sv, sv.n)
                kb.emit("v", lambda e, dsl=dsl, dk=dk: e.scalar_tensor_tensor(
                    hb[:, dsl], z2[:, dsl], c["scalec"][:, dk:dk + 1],
                    ps[0][:, :T2], ALU.mult, ALU.mult), sig=sv)
            v_h = sv.n

            kb.waittok("t", wq_ld)
            kb.waittok("t", wk_ld)
            kb.waittok("t", wv_ld)
            kb.wait("t", sv, v_h)
            own_tile = (ti * T2) < OWN
            prev_ev = {4: ("sv", 0), 5: ("sv", 0)}
            for proj, wb in (("q", wqb), ("k", wkb), ("v", wvb)):
                if proj == "q" and not own_tile:
                    continue
                for cc in range(NHP):
                    bk = 4 + cc % 2
                    pq = ps[bk]
                    pk, pv_ = prev_ev[bk]
                    kb.wait("t", sv if pk == "sv" else ss, pv_)
                    for dk in range(DK):
                        kb.emit("t", lambda e, wb=wb, cc=cc, dk=dk, pq=pq,
                                last=(dk == DK - 1): e.matmul(
                            pq[:, :T2],
                            wb[:, (dk * NHP + cc) * 128:
                               (dk * NHP + cc + 1) * 128],
                            hb[:, dk * T2:(dk + 1) * T2],
                            start=(dk == 0), stop=last,
                            skip_group_check=True),
                            sig=(sm if dk == DK - 1 else None))
                    sm_p = sm.n
                    if proj == "q":
                        kb.wait("s", sm, sm_p)
                        qw_ = min(T2, OWN - ti * T2)
                        kb.emit("s", lambda e, cc=cc, t0=ti * T2, pq=pq,
                                qw_=qw_: e.activation(
                            q_pair[:, cc * OWN + t0:cc * OWN + t0 + qw_],
                            pq[:, :qw_], AF.Identity, scale=QSC), sig=ss)
                        prev_ev[bk] = ("ss", ss.n)
                    else:
                        dst = kst if proj == "k" else vst
                        kb.wait("v", sm, sm_p)
                        kb.waittok("v", st_["kv_tok"])
                        kb.emit("v", lambda e, cc=cc, dst=dst, pq=pq:
                                e.tensor_copy(
                            dst[:, cc * T2:(cc + 1) * T2], pq[:, :T2]),
                            sig=sv)
                        prev_ev[bk] = ("sv", sv.n)
                    drain(per)
            v_kv = sv.n
            kb.wait("y", sv, v_kv)
            kb.wait("y", ss, ss.n)
            tsl = slice(ti * T2, (ti + 1) * T2)
            tk1 = dy.dma(lambda e, t=tsl: e.dma_start(
                out=io["kf_d"].rearrange("(cc p) s -> p cc s", p=128)[:, :, t],
                in_=kst[:].rearrange("p (cc t) -> p cc t", cc=NHP)))
            tk2 = dy.dma(lambda e, t=tsl: e.dma_start(
                out=io["vf_d"].rearrange("(cc p) s -> p cc s", p=128)[:, :, t],
                in_=vst[:].rearrange("p (cc t) -> p cc t", cc=NHP)))
            st_["kv_tok"] = [tk1, tk2]
            st_["sm_bnd"] = sm.n
            drain(len(pending))

        for t in make_prep(0):
            t()
        for ti in range(NT2):
            pending = make_prep(ti + 1) if ti + 1 < NT2 else []
            boundary_and_proj(ti, pending)
        kb.waittok("y", dy.token())
    c["s_q_done"] = ss.n


def build_attention(kb, io, c, q_pair):
    sm, sv, ss, sg, dy, dg = c["sm"], c["sv"], c["ss"], c["sg"], c["dy"], c["dg"]
    ps, onesrow = c["ps"], c["onesrow"]
    ptb = c["ptb"]
    NKC = SEQ // 128
    NCK = SEQ // 512
    NQT = OWN // 128
    QB = min(512, OWN)
    NQB = OWN // QB

    with contextlib.ExitStack() as stk:
        sb = lambda n, s, d: stk.enter_context(kb.nc.sbuf_tensor(n, s, d))
        k_ext = [sb(f"a_ke{i}", [65, SEQ], BF16) for i in range(2)]
        kbuf = [sb(f"a_kb{i}", [128, SEQ], BF16) for i in range(2)]
        v_ext = [sb(f"a_ve{i}", [128, NKC * 65], BF16) for i in range(2)]
        q_ext = [sb(f"a_qe{i}", [65, OWN], BF16) for i in range(2)]
        mrowT = sb("a_mr", [65, OWN], BF16)
        mrowb = [mrowT[0:1, :], mrowT[64:65, :]]
        p_tm = sb("a_ptm", [128, 4 * QB], BF16)
        m_all = [sb(f"a_ma{i}", [128, NCK + 1], F32) for i in range(2)]
        rrow = sb("a_rrow", [1, 2 * QB], F32)
        rrep = sb("a_rrep", [64, 2 * QB], F32)
        o_ev = sb("a_oev", [64, 2 * QB], BF16)

        kf = io["kf_d"]
        vf = io["vf_d"]
        od = io["o_d"]

        st_ = {
            "kbuf_tok": None, "kbuf_rd": [0, 0],
            "kext_rd": [0, 0], "qext_rd": [0, 0], "vext_rd": [0, 0],
            "mrow_v": [0, 0],
            "ptb_rd": 0,
            "a_bank_rd": [0] * 7, "a_rot": 0,
            "exp_cnt": {}, "pv_sm": {},
            "po_rd": 0, "o_tok": [None, None],
        }

        def make_pair_tasks(j):
            par = j % 2
            kbuf_ = kbuf[par]
            tasks = []

            def t_k2(j=j, par=par):
                kb.wait("g", sm, st_["kbuf_rd"][par])
                st_["kbuf_tok"] = dg.dma(lambda e, j=j: e.dma_start(
                    out=kbuf_[:, :], in_=kf[j * 128:(j + 1) * 128, :]))
            tasks.append(t_k2)

            ActR = (0, 1, 5, 6)
            for qt in range(NQT):
                for ck in range(NCK):
                    def t_a(j=j, qt=qt, ck=ck, par=par):
                        ri = (st_["a_rot"] % 2) * 2
                        st_["a_rot"] += 1
                        b0, b1 = ActR[ri], ActR[ri + 1]
                        kb.waittok("t", st_["kbuf_tok"])
                        kb.wait("t", sv, st_["a_bank_rd"][b0])
                        kb.wait("t", sv, st_["a_bank_rd"][b1])
                        qsl = slice(j * OWN + qt * 128,
                                    j * OWN + (qt + 1) * 128)
                        ksl = slice(ck * 512, (ck + 1) * 512)
                        kb.emit("t", lambda e, b0=b0: e.matmul(
                            ps[b0][:, :512], q_pair[0:64, qsl],
                            kbuf_[0:64, ksl],
                            start=True, stop=True, skip_group_check=True,
                            tile_position=(0, 0)), sig=sm)
                        kb.emit("t", lambda e, b1=b1: e.matmul(
                            ps[b1][:, :512], q_pair[64:128, qsl],
                            kbuf_[64:128, ksl],
                            start=True, stop=True, skip_group_check=True,
                            tile_position=(64, 0)), sig=sm)
                        a_sm = sm.n
                        st_["kbuf_rd"][par] = a_sm
                        kb.wait("v", sm, a_sm)
                        for hi, bb in ((0, b0), (1, b1)):
                            kb.emit("v", lambda e, hi=hi, ck=ck, bb=bb:
                                    e.reduce_max(
                                m_all[hi][:, ck:ck + 1], ps[bb][:, :512],
                                axis=AX.X), sig=sv)
                        st_["a_bank_rd"][b0] = sv.n
                        st_["a_bank_rd"][b1] = sv.n
                    tasks.append(t_a)

                for hi in range(2):
                    def t_m(hi=hi, qt=qt):
                        kb.wait("v", sv, sv.n)
                        kb.emit("v", lambda e, hi=hi: e.tensor_reduce(
                            m_all[hi][:, NCK:NCK + 1], m_all[hi][:, 0:NCK],
                            axis=AX.X, op=ALU.max, negate=True), sig=sv)
                        v_m = sv.n
                        kb.wait("t", sv, v_m)
                        kb.wait("t", sv, st_["a_bank_rd"][1])
                        kb.emit("t", lambda e, hi=hi: e.transpose(
                            ps[1][0:1, 0:128], m_all[hi][:, NCK:NCK + 1],
                            c["identf"][:, :]), sig=sm)
                        t_mt = sm.n
                        kb.wait("v", sm, t_mt)
                        kb.emit("v", lambda e, hi=hi, qt=qt: e.tensor_copy(
                            mrowb[hi][0:1, qt * 128:(qt + 1) * 128],
                            ps[1][0:1, 0:128]), sig=sv)
                        st_["a_bank_rd"][1] = sv.n
                        st_["mrow_v"][hi] = sv.n
                    tasks.append(t_m)
            return tasks

        def emit_bpv(h, pending):
            hi = h % 2
            j = h // 2
            ke = k_ext[hi]
            ve = v_ext[hi]
            qe = q_ext[hi]
            niter = NQB * (NKC + 2)
            per = (len(pending) + niter - 1) // max(1, niter)
            ti_ = [0]
            norm_q = []

            def drain(n):
                for _ in range(n):
                    if ti_[0] < len(pending):
                        pending[ti_[0]]()
                        ti_[0] += 1

            kb.wait("g", sm, st_["kext_rd"][hi])
            kext_tok = dg.dma(lambda e, h=h, hi=hi: e.dma_start(
                out=k_ext[hi][0:64, :], in_=kf[h * 64:(h + 1) * 64, :]))
            kb.emit("g", lambda e, hi=hi: e.memset(
                k_ext[hi][64:65, :], 1.0), sig=sg)
            kext_tok = [kext_tok, (sg, sg.n)]
            vband = kbuf[j % 2][0:64, :]
            kb.wait("g", sm, st_["kbuf_rd"][j % 2])
            vband_tok = dg.dma(lambda e, h=h, vband=vband: e.dma_start(
                out=vband[:, :], in_=vf[h * 64:(h + 1) * 64, :]))
            kb.emit("g", lambda e, hi=hi: e.memset(
                v_ext[hi][:].rearrange("p (a b) -> p a b", b=65)[:, :, 64:65],
                1.0), sig=sg)
            vext_ones = (sg, sg.n)
            kb.wait("t", sm, st_["vext_rd"][hi])
            for kc0 in range(0, NKC, 8):
                nb = min(8, NKC - kc0)
                kb.waittok("t", vband_tok)
                kb.wait("t", sv, st_["ptb_rd"])
                for b in range(nb):
                    kb.emit("t", lambda e, kc=kc0 + b, b=b, vband=vband:
                            e.transpose(
                        ptb[:, b * 64:(b + 1) * 64],
                        vband[:, kc * 128:(kc + 1) * 128],
                        c["identb"][0:64, 0:64]),
                        sig=(sm if b == nb - 1 else None))
                sm_tr = sm.n
                kb.wait("v", sm, sm_tr)
                kb.waittok("v", [vext_ones])
                for b in range(nb):
                    kb.emit("v", lambda e, hi=hi, kc=kc0 + b, b=b:
                            e.tensor_copy(
                        v_ext[hi][:, kc * 65:kc * 65 + 64],
                        ptb[:, b * 64:(b + 1) * 64]),
                        sig=(sv if b == nb - 1 else None))
                st_["ptb_rd"] = sv.n
            st_["kbuf_rd"][j % 2] = sm.n
            v_vext = sv.n
            kb.wait("v", sm, st_["qext_rd"][hi])
            kb.wait("v", sv, st_["mrow_v"][hi])
            kb.emit("v", lambda e, hi=hi, j=j: e.tensor_copy(
                q_ext[hi][0:64, :],
                q_pair[hi * 64:hi * 64 + 64, j * OWN:(j + 1) * OWN]), sig=sv)
            kb.emit("v", lambda e, hi=hi: e.tensor_copy(
                q_ext[hi][64:65, :], mrowb[hi][0:1, :]), sig=sv)
            v_qx = sv.n

            kb.waittok("t", kext_tok)
            kb.wait("t", sv, v_qx)
            kb.wait("t", sv, v_vext)
            kb.wait("t", ss, ss.n)
            kb.wait("s", sm, sm.n)
            st_["exp_cnt"] = {}
            st_["pv_sm"] = {}
            for qb in range(NQB):
                q0 = qb * QB
                po = ps[4]
                for step in range(NKC + 2):
                    if step < NKC:
                        kc = step
                        pb = ps[2 + kc % 2]
                        if kc >= 2:
                            kb.wait("t", ss, st_["exp_cnt"][(qb, kc - 2)])
                        elif qb > 0:
                            kb.wait("t", ss,
                                    st_["exp_cnt"][(qb - 1, NKC - 2 + kc)])
                        kb.emit("t", lambda e, kc=kc, q0=q0, pb=pb: e.matmul(
                            pb[:, :QB], ke[:, kc * 128:(kc + 1) * 128],
                            qe[:, q0:q0 + QB],
                            start=True, stop=True, skip_group_check=True),
                            sig=sm)
                        sm_b = sm.n
                        kb.wait("s", sm, sm_b)
                        pkey = (qb, kc - 4) if kc >= 4 else (
                            (qb - 1, NKC - 4 + kc) if qb > 0 else None)
                        if pkey in st_["pv_sm"]:
                            kb.wait("s", sm, st_["pv_sm"][pkey])
                        kb.emit("s", lambda e, kc=kc, pb=pb: e.activation(
                            p_tm[:, (kc % 4) * QB:(kc % 4 + 1) * QB],
                            pb[:, :QB], AF.Exp), sig=ss)
                        st_["exp_cnt"][(qb, kc)] = ss.n
                    if step >= 2:
                        kc = step - 2
                        kb.wait("t", ss, st_["exp_cnt"][(qb, kc)])
                        if kc == 0:
                            kb.wait("t", sv, st_["po_rd"])
                        is_last = (kc == NKC - 1)
                        kb.emit("t", lambda e, kc=kc, po=po,
                                first=(kc == 0), last=is_last: e.matmul(
                            po[0:65, :QB], ve[:, kc * 65:(kc + 1) * 65],
                            p_tm[:, (kc % 4) * QB:(kc % 4 + 1) * QB],
                            start=first, stop=last, skip_group_check=True),
                            sig=sm)
                        st_["pv_sm"][(qb, kc)] = sm.n
                    drain(per)
                sm_pv = sm.n

                def normalize(qb=qb, po=po, sm_pv=sm_pv, h=h):
                    par = qb % 2
                    kb.wait("v", sm, sm_pv)
                    kb.waittok("v", st_["o_tok"][par])
                    kb.emit("v", lambda e: e.reciprocal(
                        rrow[0:1, par * QB:par * QB + QB], po[64:65, :QB]),
                        sig=sv)
                    v_rr = sv.n
                    kb.wait("t", sv, v_rr)
                    kb.wait("t", sv, st_["a_bank_rd"][6])
                    kb.emit("t", lambda e: e.matmul(
                        ps[6][0:64, :QB], onesrow[0:1, 0:64],
                        rrow[0:1, par * QB:par * QB + QB],
                        start=True, stop=True, skip_group_check=True), sig=sm)
                    sm_rr = sm.n
                    kb.wait("v", sm, sm_rr)
                    kb.emit("v", lambda e: e.tensor_copy(
                        rrep[:, par * QB:par * QB + QB], ps[6][0:64, :QB]),
                        sig=sv)
                    st_["a_bank_rd"][6] = sv.n
                    kb.wait("v", sv, sv.n)
                    kb.emit("v", lambda e: e.tensor_tensor(
                        o_ev[:, par * QB:par * QB + QB], po[0:64, :QB],
                        rrep[:, par * QB:par * QB + QB], ALU.mult), sig=sv)
                    v_oev = sv.n
                    st_["po_rd"] = v_oev
                    kb.wait("y", sv, v_oev)
                    st_["o_tok"][par] = dy.dma(
                        lambda e, h=h, q0=qb * QB, par=par: e.dma_start(
                            out=od[h * 64:(h + 1) * 64, q0:q0 + QB],
                            in_=o_ev[:, par * QB:par * QB + QB]))
                normalize()
            drain(len(pending))
            st_["kext_rd"][hi] = sm.n
            st_["qext_rd"][hi] = sm.n
            st_["vext_rd"][hi] = sm.n

        for t in make_pair_tasks(0):
            t()
        for j in range(NH // 2):
            if j + 1 < NH // 2:
                pending = make_pair_tasks(j + 1)
            else:
                pending = []
            half = len(pending) // 2
            emit_bpv(2 * j, pending[:half])
            emit_bpv(2 * j + 1, pending[half:])
        kb.waittok("y", dy.token())


def build_outproj(kb, io, c):
    sm, sv, ss, sg, dy, dg = c["sm"], c["sv"], c["ss"], c["sg"], c["dy"], c["dg"]
    ps = c["ps"]
    HD = NH * DH
    NHP = HD // 128
    T2 = min(1024, OWN)
    NT2 = OWN // T2
    with contextlib.ExitStack() as stk:
        sb = lambda n, s, d: stk.enter_context(kb.nc.sbuf_tensor(n, s, d))
        ot = sb("o_ot", [128, NHP * T2], BF16)
        yt = sb("o_yt", [128, DK * T2], F32)
        tv = sb("o_tv", [128, 2 * T2], F32)
        y2 = sb("o_y2", [128, DK * T2], F32)
        wob = sb("o_wo", [128, NHP * DK * 128], BF16)

        wo_ld = dg.dma(lambda e: e.dma_start(out=wob[:], in_=io["wo_t"][:]))
        y1_r = io["y1_d"].rearrange("(dk p) s -> p dk s", p=128)
        y2_r = io["y2_d"].rearrange("(dk p) s -> p dk s", p=128)
        od_r = io["o_d"].rearrange("(cc p) s -> p cc s", p=128)
        prev_v = 0
        prev_store = None
        for ti in range(NT2):
            tsl = slice(ti * T2, (ti + 1) * T2)
            kb.wait("y", sv, prev_v)
            o_load = dy.dma(lambda e, t=tsl: e.dma_start(
                out=ot[:].rearrange("p (cc t) -> p cc t", cc=NHP),
                in_=od_r[:, :, t]))
            y_load = dy.dma(lambda e, t=tsl: e.dma_start(
                out=yt[:].rearrange("p (dk t) -> p dk t", dk=DK),
                in_=y1_r[:, :, t]))
            kb.waittok("t", o_load)
            kb.waittok("t", wo_ld)
            HW2 = min(512, T2)
            NHF2 = T2 // HW2
            for dm in range(DK):
                pqs = (ps[4], ps[5]) if dm % 2 == 0 else (ps[0], ps[1])
                kb.wait("t", ss, ss.n)
                for half in range(NHF2):
                    for cc in range(NHP):
                        kb.emit("t", lambda e, cc=cc, dm=dm, half=half,
                                pq=pqs[half]: e.matmul(
                            pq[:, :HW2],
                            wob[:, (cc * DK + dm) * 128:(cc * DK + dm + 1) * 128],
                            ot[:, cc * T2 + half * HW2:
                               cc * T2 + (half + 1) * HW2],
                            start=(cc == 0), stop=(cc == NHP - 1),
                            skip_group_check=True),
                            sig=(sm if cc == NHP - 1 and half == NHF2 - 1
                                 else None))
                sm_o = sm.n
                kb.wait("s", sm, sm_o)
                kb.wait("s", sv, sv.n)
                for half in range(NHF2):
                    kb.emit("s", lambda e, dm=dm, half=half, pq=pqs[half]:
                            e.activation(
                        tv[:, (dm % 2) * T2 + half * HW2:
                           (dm % 2) * T2 + (half + 1) * HW2], pq[:, :HW2],
                        AF.Identity, bias=c["boutc"][:, dm:dm + 1]), sig=ss)
                s_tv = ss.n
                kb.waittok("v", y_load)
                kb.wait("v", ss, s_tv)
                kb.wait("v", sv, sv.n)
                kb.emit("v", lambda e, dm=dm: e.tensor_tensor(
                    y2[:, dm * T2:(dm + 1) * T2],
                    tv[:, (dm % 2) * T2:(dm % 2 + 1) * T2],
                    yt[:, dm * T2:(dm + 1) * T2], ALU.add), sig=sv)
            v_y2 = sv.n
            prev_v = v_y2
            kb.wait("y", sv, v_y2)
            kb.waittok("y", prev_store)
            prev_store = dy.dma(lambda e, t=tsl: e.dma_start(
                out=y2_r[:, :, t],
                in_=y2[:].rearrange("p (dk t) -> p dk t", dk=DK)))
        kb.waittok("y", dy.token())


def build_program():
    kb = KB()
    io = declare_io(kb)
    c = declare_common(kb, io)
    build_prepass(kb, io, c)
    kb.barrier()
    build_ffn(kb, io, c, src=io["x_fm"], dst=io["y1_d"], ntiles=SEQ // TT,
              first=True)
    kb.barrier()
    with kb.nc.sbuf_tensor("q_pair", [128, (NH // 2) * OWN], BF16) as q_pair:
        build_qkv(kb, io, c, q_pair)
        kb.barrier()
        build_attention(kb, io, c, q_pair)
        kb.barrier()
    build_outproj(kb, io, c)
    kb.barrier()
    build_ffn(kb, io, c, src=io["y2_d"], dst=io["out_fm"], ntiles=OWN // TT)
    kb.barrier()
    kb.finalize()
    kb.es.close()
    return kb.nc


def kernel(**inputs):
    in_maps = host_prepare(inputs)
    nc = build_program()
    res = run_bass_kernel_spmd(nc, in_maps, core_ids=list(range(8)))
    return host_assemble(res.results)


# version 15
# speedup vs baseline: 1.0920x; 1.0920x over previous
import contextlib
import numpy as np

import concourse.bass as bass
import concourse.mybir as mybir
from concourse.bass_utils import run_bass_kernel_spmd

F32 = mybir.dt.float32
BF16 = mybir.dt.bfloat16
AF = mybir.ActivationFunctionType
ALU = mybir.AluOpType
AX = mybir.AxisListType

D = 1024
DK = D // 128
SEQ = 8192
OWN = 4096
TT = 512
E = 8
HF = 4096
NH = 16
DH = 64
NEG_BIG = -1e30
LN_EPS = 1e-5
RMS_EPS = 1e-6


class Sem:
    def __init__(self, h):
        self.h = h
        self.n = 0


class DmaQ:

    def __init__(self, kb, eng, name, K=4):
        self.kb = kb
        self.eng = eng
        self.lanes = [kb.sem(f"{name}{i}") for i in range(K)]
        self.K = K
        self.i = 0

    def dma(self, fn):
        lane = self.lanes[self.i % self.K]
        self.i += 1
        if lane.n > 0:
            self.kb.wait(self.eng, lane, lane.n)
        n = self.kb.emit(self.eng, fn, sig=lane, dma=True)
        return (lane, n)

    def token(self):
        return [(l, l.n) for l in self.lanes if l.n > 0]


class KB:

    ENGS = ("t", "v", "s", "g", "y")

    def __init__(self):
        self.nc = bass.Bass("TRN2", target_bir_lowering=False)
        self.es = contextlib.ExitStack()
        self.S = {e: [] for e in self.ENGS}

    def sem(self, name):
        return Sem(self.es.enter_context(self.nc.semaphore(name)))

    def sbuf(self, name, shp, dt):
        return self.es.enter_context(self.nc.sbuf_tensor(name, shp, dt))

    def psum(self, name, shp, dt=F32):
        return self.es.enter_context(self.nc.psum_tensor(name, shp, dt))

    def emit(self, eng, fn, sig=None, dma=False):
        if sig is None:
            self.S[eng].append(fn)
            return None
        k = 16 if dma else 1
        sig.n += k
        self.S[eng].append(lambda e, fn=fn, s=sig, k=k: fn(e).then_inc(s.h, k))
        return sig.n

    def wait(self, eng, sem, val):
        if val <= 0:
            return
        key = (eng, id(sem))
        if not hasattr(self, "_lw"):
            self._lw = {}
        if self._lw.get(key, -1) >= val:
            return
        self._lw[key] = val
        self.S[eng].append(lambda e, s=sem, v=val: e.wait_ge(s.h, v))

    def waittok(self, eng, tok):
        if tok is None:
            return
        if isinstance(tok, list):
            for t in tok:
                self.waittok(eng, t)
            return
        sem, val = tok
        self.wait(eng, sem, val)

    def barrier(self):
        for q in getattr(self, "dmaqs", ()):
            self.waittok(q.eng, q.token())
        if not hasattr(self, "_bar"):
            self._bar = {e: self.sem(f"bar_{e}") for e in self.ENGS}
        for e in self.ENGS:
            s = self._bar[e]
            s.n += 1
            self.S[e].append(lambda en, s=s: en.nop().then_inc(s.h, 1))
        for e in self.ENGS:
            for o in self.ENGS:
                if o != e:
                    self.wait(e, self._bar[o], self._bar[o].n)

    def finalize(self):
        with self.nc.Block() as block:
            @block.tensor
            def _(t):
                for f in self.S["t"]:
                    f(t)

            @block.vector
            def _(v):
                for f in self.S["v"]:
                    f(v)

            @block.scalar
            def _(s):
                for f in self.S["s"]:
                    f(s)

            @block.gpsimd
            def _(g):
                for f in self.S["g"]:
                    f(g)

            @block.sync
            def _(y):
                for f in self.S["y"]:
                    f(y)



def _tile_w(w):
    din, dout = w.shape
    t = w.reshape(din // 128, 128, dout // 128, 128)
    return np.ascontiguousarray(t.transpose(1, 0, 2, 3))


def _tile_w_om(w):
    din, dout = w.shape
    t = w.reshape(din // 128, 128, dout // 128, 128)
    return np.ascontiguousarray(t.transpose(1, 2, 0, 3))


def host_prepare(inputs):
    HC = HF // 128
    x = np.asarray(inputs["L0_attn_x"], dtype=np.float32)
    wq = np.asarray(inputs["L0_attn_wq"], np.float32).reshape(D, NH * DH)
    wk = np.asarray(inputs["L0_attn_wk"], np.float32).reshape(D, NH * DH)
    wv = np.asarray(inputs["L0_attn_wv"], np.float32).reshape(D, NH * DH)
    w_out = np.asarray(inputs["L0_attn_w_out"], np.float32).reshape(NH * DH, D)
    norm_scale = np.asarray(inputs["L0_attn_norm_scale"], np.float32)
    b_out = np.asarray(inputs["L0_attn_b_out"], np.float32)
    rw1 = np.asarray(inputs["L0_ffn_rw1"], np.float32)
    rb1 = np.asarray(inputs["L0_ffn_rb1"], np.float32)
    rw2 = np.asarray(inputs["L0_ffn_rw2"], np.float32)
    rb2 = np.asarray(inputs["L0_ffn_rb2"], np.float32)
    we1 = np.asarray(inputs["L0_ffn_we1"], np.float32)
    be1 = np.asarray(inputs["L0_ffn_be1"], np.float32)
    we2 = np.asarray(inputs["L0_ffn_we2"], np.float32)
    be2 = np.asarray(inputs["L0_ffn_be2"], np.float32)

    rb2_col = np.zeros((128, 1), np.float32)
    rb2_col[:E, 0] = rb2
    be2_row = np.zeros((8, D), np.float32)
    be2_row[:E] = be2

    shared = dict(
        we1_t=np.stack([_tile_w_om(we1[e]).reshape(128, -1) for e in range(E)]),
        we2_t=np.stack([_tile_w(we2[e]).reshape(128, -1) for e in range(E)]),
        rw1_t=_tile_w(rw1).reshape(128, -1),
        wq_t=_tile_w(wq).reshape(128, -1),
        wk_t=_tile_w(wk).reshape(128, -1),
        wv_t=_tile_w(wv).reshape(128, -1),
        wo_t=_tile_w(w_out).reshape(128, -1),
        be1_t=np.ascontiguousarray(be1.reshape(E, HC, 128).transpose(0, 2, 1)),
        be2_row=be2_row,
        rb1_t=np.ascontiguousarray(rb1.reshape(DK, 128).T),
        scale_t=np.ascontiguousarray(norm_scale.reshape(DK, 128).T),
        bout_t=np.ascontiguousarray(b_out.reshape(DK, 128).T),
        rb2_col=rb2_col,
        rw2_t=np.ascontiguousarray(
            rw2.reshape(DK, 128, E).transpose(1, 0, 2)).reshape(128, DK * E),
        ident=np.eye(128, dtype=np.float32),
        esel=np.ascontiguousarray(
            np.eye(E, dtype=np.float32)[:, :, None].repeat(128, 2).reshape(
                E, E * 128)),
    )

    in_maps = []
    for c in range(2 * x.shape[0]):
        b, half = c // 2, c % 2
        off = half * OWN
        xb = x[b]
        xp = np.concatenate([xb[off:off + OWN], xb[OWN - off:2 * OWN - off]],
                            axis=0)
        m = dict(shared)
        m["x_fm"] = np.ascontiguousarray(xp.T)
        in_maps.append(m)
    return in_maps


def host_assemble(results):
    out = np.empty((len(results) // 2, SEQ, D), np.float32)
    for c in range(len(results)):
        b, half = c // 2, c % 2
        off = half * OWN
        out[b, off:off + OWN] = results[c]["out_fm"].T
    return out



def declare_io(kb):
    nc = kb.nc
    HC = HF // 128
    HD = NH * DH
    ap = lambda name, shp, dt=F32, kind="ExternalInput": nc.dram_tensor(
        name, shp, dt, kind=kind).ap()
    io = {}
    io["x_fm"] = ap("x_fm", [D, SEQ])
    io["we1_t"] = ap("we1_t", [E, 128, HC * DK * 128])
    io["we2_t"] = ap("we2_t", [E, 128, HC * DK * 128])
    io["rw1_t"] = ap("rw1_t", [128, DK * DK * 128])
    io["wq_t"] = ap("wq_t", [128, DK * (HD // 128) * 128])
    io["wk_t"] = ap("wk_t", [128, DK * (HD // 128) * 128])
    io["wv_t"] = ap("wv_t", [128, DK * (HD // 128) * 128])
    io["wo_t"] = ap("wo_t", [128, (HD // 128) * DK * 128])
    io["be1_t"] = ap("be1_t", [E, 128, HC])
    io["be2_row"] = ap("be2_row", [8, D])
    io["rb1_t"] = ap("rb1_t", [128, DK])
    io["scale_t"] = ap("scale_t", [128, DK])
    io["bout_t"] = ap("bout_t", [128, DK])
    io["rb2_col"] = ap("rb2_col", [128, 1])
    io["rw2_t"] = ap("rw2_t", [128, DK * E])
    io["ident"] = ap("ident", [128, 128])
    io["esel"] = ap("esel", [E, E * 128])
    io["out_fm"] = ap("out_fm", [D, OWN], kind="ExternalOutput")
    iap = lambda name, shp, dt: nc.dram_tensor(name, shp, dt, kind="Internal").ap()
    io["we1_b"] = iap("we1_b", [E, 128, HC * DK * 128], BF16)
    io["we2_b"] = iap("we2_b", [E, 128, HC * DK * 128], BF16)
    io["rw1_hi"] = iap("rw1_hi", [128, DK * DK * 128], BF16)
    io["rw1_lo"] = iap("rw1_lo", [128, DK * DK * 128], BF16)
    io["y1_d"] = iap("y1_d", [D, SEQ], F32)
    io["y2_d"] = iap("y2_d", [D, OWN], F32)
    io["kf_d"] = iap("kf_d", [HD, SEQ], BF16)
    io["vf_d"] = iap("vf_d", [HD, SEQ], BF16)
    io["o_d"] = iap("o_d", [HD, OWN], BF16)
    return io


def declare_common(kb, io):
    c = {}
    c["onesrow"] = kb.sbuf("onesrow", [1, 128], F32)
    c["onescol"] = kb.sbuf("onescol", [128, 1], F32)
    c["identf"] = kb.sbuf("identf", [128, 128], F32)
    c["identb"] = kb.sbuf("identb", [128, 128], BF16)
    c["rb2c"] = kb.sbuf("rb2c", [128, 1], F32)
    c["rb1c"] = kb.sbuf("rb1c", [128, DK], F32)
    c["scalec"] = kb.sbuf("scalec", [128, DK], F32)
    c["boutc"] = kb.sbuf("boutc", [128, DK], F32)
    c["be1c"] = kb.sbuf("be1c", [128, E * (HF // 128)], F32)
    c["be2r"] = kb.sbuf("be2r", [8, D], F32)
    c["rw2c"] = kb.sbuf("rw2c", [128, DK * E], F32)
    c["esel"] = kb.sbuf("eselc", [E, E * 128], F32)
    c["cst"] = kb.sbuf("cst", [128, 4], F32)
    c["rw2h"] = kb.sbuf("rw2h", [128, DK * E], BF16)
    c["rw2l"] = kb.sbuf("rw2l", [128, DK * E], BF16)
    c["ps"] = [kb.psum(f"ps{i}", [128, 512]) for i in range(7)]
    c["ptb"] = kb.psum("ptb", [128, 1024], BF16)
    for s in ("sm", "sv", "ss", "sg"):
        c[s] = kb.sem(s)
    c["dg"] = DmaQ(kb, "g", "dg", K=4)
    c["dy"] = DmaQ(kb, "y", "dy", K=4)
    c["dw"] = DmaQ(kb, "y", "dw", K=6)
    kb.dmaqs = (c["dg"], c["dy"], c["dw"])
    return c


def build_prepass(kb, io, c):
    sv, sg, dg, dy = c["sv"], c["sg"], c["dg"], c["dy"]
    for dst, src in (
        (c["identf"], io["ident"]), (c["rb2c"], io["rb2_col"]),
        (c["rb1c"], io["rb1_t"]), (c["scalec"], io["scale_t"]),
        (c["boutc"], io["bout_t"]), (c["be2r"], io["be2_row"]),
        (c["rw2c"], io["rw2_t"]), (c["esel"], io["esel"]),
    ):
        dy.dma(lambda e, d=dst, s=src: e.dma_start(out=d[:], in_=s[:]))
    dy.dma(lambda e: e.dma_start(
        out=c["be1c"][:].rearrange("p (e h) -> p e h", e=E),
        in_=io["be1_t"].rearrange("e p h -> p e h")))
    tok_smalls = dy.token()
    kb.emit("g", lambda e: e.memset(c["onesrow"][:], 1.0), sig=sg)
    kb.emit("g", lambda e: e.memset(c["onescol"][:], 1.0), sig=sg)
    kb.emit("g", lambda e: e.memset(c["cst"][:, 0:1], LN_EPS), sig=sg)
    kb.emit("g", lambda e: e.memset(c["cst"][:, 1:2], RMS_EPS), sig=sg)
    c["sg_ones"] = sg.n
    kb.waittok("v", tok_smalls)
    kb.emit("v", lambda e: e.tensor_copy(c["identb"][:], c["identf"][:]), sig=sv)
    kb.emit("v", lambda e: e.tensor_copy(c["rw2h"][:], c["rw2c"][:]), sig=sv)
    kb.wait("v", sv, sv.n)
    kb.emit("v", lambda e: e.tensor_tensor(
        c["rw2l"][:], c["rw2c"][:], c["rw2h"][:], ALU.subtract), sig=sv)
    c["n_small_v"] = sv.n
    c["tok_smalls"] = tok_smalls

    cw = DK * DK * 128
    with kb.nc.sbuf_tensor("cb_f", [128, cw], F32) as cb_f, \
         kb.nc.sbuf_tensor("cb_h", [128, cw], BF16) as cb_h, \
         kb.nc.sbuf_tensor("cb_l", [128, cw], BF16) as cb_l:
        t_in = dg.dma(lambda e: e.dma_start(out=cb_f[:], in_=io["rw1_t"][:]))
        kb.waittok("v", t_in)
        kb.emit("v", lambda e: e.tensor_copy(cb_h[:], cb_f[:]), sig=sv)
        kb.wait("v", sv, sv.n)
        kb.emit("v", lambda e: e.tensor_tensor(
            cb_l[:], cb_f[:], cb_h[:], ALU.subtract), sig=sv)
        v_done = sv.n
        kb.wait("g", sv, v_done)
        dg.dma(lambda e: e.dma_start(out=io["rw1_hi"][:], in_=cb_h[:]))
        dg.dma(lambda e: e.dma_start(out=io["rw1_lo"][:], in_=cb_l[:]))
    c["tok_rw1"] = dg.token()


_FFN_N = [0]


def build_ffn(kb, io, c, src, dst, ntiles, first=False):
    sm, sv, ss, sg, dg, dy = c["sm"], c["sv"], c["ss"], c["sg"], c["dg"], c["dy"]
    ps, onesrow, onescol = c["ps"], c["onesrow"], c["onescol"]
    HC = HF // 128
    NC4 = TT // 128
    GHC = 2
    NS = 4
    NGRP = HC // GHC
    GW = GHC * DK * 128

    _FFN_N[0] += 1
    tag = f"F{_FFN_N[0]}"
    with contextlib.ExitStack() as stk:
        sb = lambda n, s, d: stk.enter_context(kb.nc.sbuf_tensor(tag + n, s, d))
        xt = sb("f_xt", [128, DK * TT], F32)
        zh2 = sb("f_zh", [128, 2 * DK * TT], BF16)
        zl = sb("f_zl", [128, DK * TT], BF16)
        zf = sb("f_zf", [128, TT], F32)
        acc = sb("f_acc", [128, DK * TT], F32)
        h_sb = sb("f_h", [128, HC * TT], BF16)
        w1b = sb("f_w1b", [128, NS * GW], BF16)
        w2b = sb("f_w2b", [128, NS * GW], BF16)
        r1f = sb("f_r1f", [128, 2 * TT], F32)
        sq = r1f
        r1h = sb("f_r1h", [128, DK * TT], BF16)
        r1l = sb("f_r1l", [128, DK * TT], BF16)
        wbc = sb("f_wbc", [128, 2 * TT], F32)
        eot = sb("f_eot", [128, 2 * TT], F32)
        rows = sb("f_rows", [1, 9 * TT], F32)
        lg8 = sb("f_lg8", [128, E], F32)
        l28 = sb("f_l28", [128, E], F32)
        l38 = sb("f_l38", [128, E], F32)
        m1x8 = sb("f_m1x8", [128, E], F32)
        m2x8 = sb("f_m2x8", [128, E], F32)
        lgs = sb("f_lgs", [8, TT], F32)
        m12 = sb("f_m12", [128, 8 * NC4], F32)
        tp8 = sb("f_tp8", [128, 4 * E], F32)
        wtm = sb("f_wtm", [128, NC4 * E], F32)
        wfm = sb("f_wfm", [8, TT], F32)
        if first:
            qw = HC * DK * 128 // 16
            cvt = sb("f_cvt", [128, 2 * qw], BF16)
            conv_state = {"out_done": [None, None]}
            c["tok_exp"] = [None] * E

            def emit_convert(e_i):
                dgq = c["dg"]
                for mat_in, mat_out in ((io["we1_t"], io["we1_b"]),
                                        (io["we2_t"], io["we2_b"])):
                    for q in range(16):
                        h2 = q % 2
                        kb.waittok("g", conv_state["out_done"][h2])
                        t_in = dgq.dma(lambda e, m=mat_in, ei=e_i, q=q, h2=h2:
                                       e.dma_start(
                            out=cvt[:, h2 * qw:(h2 + 1) * qw],
                            in_=m[ei, :, q * qw:(q + 1) * qw]))
                        kb.waittok("g", t_in)
                        conv_state["out_done"][h2] = dgq.dma(
                            lambda e, m=mat_out, ei=e_i, q=q, h2=h2:
                            e.dma_start(
                                out=m[ei, :, q * qw:(q + 1) * qw],
                                in_=cvt[:, h2 * qw:(h2 + 1) * qw]))
                c["tok_exp"][e_i] = c["dg"].token()

        kb.waittok("g", c["tok_rw1"])

        row_sl = lambda i: rows[0:1, i * TT:(i + 1) * TT]
        R_MEAN, R_MS, R_SQM, R_VAR, R_SD, R_RSTD, R_B, R_A2, R_B2 = range(9)

        src_r = src.rearrange("(dk p) s -> p dk s", p=128)
        dst_r = dst.rearrange("(dk p) s -> p dk s", p=128)

        st_ = {
            "acc_dma": None,
            "v_accinit": 0,
            "s_sq": 0,
            "x_tok": None,
            "v_z": 0,
            "sm_bnd": 0,
            "g_acc": [0] * DK,
            "g_eot": [0, 0],
            "w1_rd": [0] * NS,
            "w2_rd": [0] * NS,
            "eo_rd": {},
            "v_wbc": 0,
            "v_wfm": 0,
            "rw1_tok": None,
            "sm_we_last": 0,
        }

        def make_prep(ti):
            par = ti % 2
            zh = zh2[:, par * DK * TT:(par + 1) * DK * TT]
            tasks = []
            tsl = slice(ti * TT, (ti + 1) * TT)

            def t_xload():
                kb.wait("y", sv, st_["v_accinit"])
                kb.wait("y", ss, st_["s_sq"])
                st_["x_tok"] = dy.dma(lambda e, t=tsl: e.dma_start(
                    out=xt[:].rearrange("p (dk t) -> p dk t", dk=DK),
                    in_=src_r[:, :, t]))
            tasks.append(t_xload)

            def t_sum():
                kb.waittok("t", st_["x_tok"])
                kb.wait("t", sv, st_["v_accinit"])
                for dk in range(DK):
                    kb.emit("t", lambda e, dk=dk, last=(dk == DK - 1):
                            e.matmul(
                        ps[1][0:1, :TT], onescol[:, 0:1],
                        xt[:, dk * TT:(dk + 1) * TT],
                        start=(dk == 0), stop=last, skip_group_check=True),
                        sig=(sm if dk == DK - 1 else None))
                sm_sum = sm.n
                kb.wait("v", sm, sm_sum)
                kb.wait("v", sm, st_["sm_bnd"])
                kb.emit("v", lambda e: e.tensor_scalar(
                    row_sl(R_MEAN), ps[1][0:1, :TT], 1.0 / D, None, ALU.mult),
                    sig=sv)
            tasks.append(t_sum)

            def t_sq(dk):
                kb.waittok("s", st_["x_tok"])
                if dk >= 2:
                    kb.wait("s", sm, st_["_sm_sq2"])
                kb.emit("s", lambda e, dk=dk: e.activation(
                    sq[:, (dk % 2) * TT:(dk % 2 + 1) * TT],
                    xt[:, dk * TT:(dk + 1) * TT], AF.Square), sig=ss)
                st_["s_sq"] = ss.n
                kb.wait("t", ss, ss.n)
                if dk == 0:
                    kb.wait("t", sv, sv.n)
                kb.emit("t", lambda e, dk=dk, last=(dk == DK - 1): e.matmul(
                    ps[1][0:1, :TT], onescol[:, 0:1],
                    sq[:, (dk % 2) * TT:(dk % 2 + 1) * TT],
                    start=(dk == 0), stop=last, skip_group_check=True), sig=sm)
                st_["_sm_sq2"] = sm.n
            for dk in range(DK):
                tasks.append(lambda dk=dk: t_sq(dk))

            def t_chain():
                kb.wait("v", sm, st_["_sm_sq2"])
                kb.emit("v", lambda e: e.tensor_scalar(
                    row_sl(R_MS), ps[1][0:1, :TT], 1.0 / D, None, ALU.mult),
                    sig=sv)
                v_ms = sv.n
                kb.wait("s", sv, v_ms)
                kb.emit("s", lambda e: e.activation(
                    row_sl(R_SQM), row_sl(R_MEAN), AF.Square), sig=ss)
                s_sqm = ss.n
                kb.wait("v", ss, s_sqm)
                kb.emit("v", lambda e: e.tensor_tensor(
                    row_sl(R_VAR), row_sl(R_MS), row_sl(R_SQM), ALU.subtract),
                    sig=sv)
                v_var = sv.n
                kb.wait("s", sv, v_var)
                kb.wait("s", c["sg"], c["sg_ones"])
                kb.emit("s", lambda e: e.activation(
                    row_sl(R_SD), row_sl(R_VAR), AF.Sqrt,
                    bias=c["cst"][0:1, 0:1]), sig=ss)
                s_sd = ss.n
                kb.wait("v", ss, s_sd)
                kb.emit("v", lambda e: e.reciprocal(
                    row_sl(R_RSTD), row_sl(R_SD)), sig=sv)
                kb.wait("v", sv, sv.n)
                kb.emit("v", lambda e: e.tensor_tensor(
                    row_sl(R_B), row_sl(R_MEAN), row_sl(R_RSTD), ALU.mult),
                    sig=sv)
                kb.emit("v", lambda e: e.tensor_scalar(
                    row_sl(R_A2), row_sl(R_RSTD), 0.5, 1.0, ALU.mult, ALU.add),
                    sig=sv)
                kb.wait("v", sv, sv.n)
                kb.emit("v", lambda e: e.tensor_scalar(
                    row_sl(R_B2), row_sl(R_B), 0.5, None, ALU.mult), sig=sv)
                st_["_v_rows"] = sv.n
            tasks.append(t_chain)

            def t_bc():
                kb.wait("t", sv, st_["_v_rows"])
                kb.emit("t", lambda e: e.matmul(
                    ps[1][:, :TT], onesrow[:, :], row_sl(R_RSTD),
                    start=True, stop=True, skip_group_check=True), sig=sm)
                kb.emit("t", lambda e: e.matmul(
                    ps[3][:, :TT], onesrow[:, :], row_sl(R_B),
                    start=True, stop=True, skip_group_check=True), sig=sm)
                st_["_sm_bc"] = sm.n
            tasks.append(t_bc)

            def t_z(dk):
                dsl = slice(dk * TT, (dk + 1) * TT)
                kb.wait("v", sm, st_["_sm_bc"])
                kb.wait("v", sm, st_["sm_bnd"])
                kb.wait("v", sv, sv.n)
                kb.emit("v", lambda e, dsl=dsl: e.tensor_tensor(
                    zf[:], xt[:, dsl], ps[1][:, :TT], ALU.mult), sig=sv)
                kb.wait("v", sv, sv.n)
                kb.emit("v", lambda e, dsl=dsl: e.tensor_tensor(
                    zh[:, dsl], zf[:], ps[3][:, :TT], ALU.subtract), sig=sv)
                kb.emit("v", lambda e: e.tensor_tensor(
                    sq[:, TT:2 * TT], zf[:], ps[3][:, :TT], ALU.subtract),
                    sig=sv)
                kb.wait("v", sv, sv.n)
                kb.emit("v", lambda e, dsl=dsl: e.tensor_tensor(
                    zl[:, dsl], sq[:, TT:2 * TT], zh[:, dsl], ALU.subtract),
                    sig=sv)
                st_["v_z"] = sv.n
            for dk in range(DK):
                tasks.append(lambda dk=dk: t_z(dk))
            return tasks

        def boundary(ti):
            par = ti % 2
            zh = zh2[:, par * DK * TT:(par + 1) * DK * TT]
            kb.wait("g", sm, max(st_["w1_rd"] + st_["w2_rd"]))
            st_["rw1_tok"] = [
                dg.dma(lambda e: e.dma_start(out=w1b[:], in_=io["rw1_hi"][:])),
                dg.dma(lambda e: e.dma_start(out=w2b[:], in_=io["rw1_lo"][:]))]
            kb.wait("t", sv, st_["v_z"])
            kb.wait("t", sv, st_["v_wbc"])
            kb.emit("t", lambda e: e.matmul(
                ps[2][:, :TT], onesrow[:, :], row_sl(R_A2),
                start=True, stop=True, skip_group_check=True), sig=sm)
            kb.emit("t", lambda e: e.matmul(
                ps[3][:, :TT], onesrow[:, :], row_sl(R_B2),
                start=True, stop=True, skip_group_check=True), sig=sm)
            sm_bc2 = sm.n
            kb.wait("v", sm, sm_bc2)
            kb.waittok("v", st_["acc_dma"])
            for dk in range(DK):
                dsl = slice(dk * TT, (dk + 1) * TT)
                kb.wait("v", sv, sv.n)
                kb.emit("v", lambda e, dsl=dsl: e.tensor_tensor(
                    zf[:], xt[:, dsl], ps[2][:, :TT], ALU.mult), sig=sv)
                kb.wait("v", sv, sv.n)
                kb.wait("v", c["sg"], st_["g_acc"][dk])
                kb.emit("v", lambda e, dsl=dsl: e.tensor_tensor(
                    acc[:, dsl], zf[:], ps[3][:, :TT], ALU.subtract), sig=sv)
            st_["v_accinit"] = sv.n

            kb.waittok("t", st_["rw1_tok"])
            terms = ((w1b, zh), (w1b, zl), (w2b, zh))
            s_r1_start = ss.n
            for dm in range(DK):
                pr = ps[4 + dm % 2]
                if dm >= 2:
                    kb.wait("t", ss, s_r1_start + dm - 1)
                nmm = 0
                for (wsrc, zsrc) in terms:
                    for dk in range(DK):
                        nmm += 1
                        kb.emit("t", lambda e, w=wsrc, z=zsrc, dm=dm, dk=dk,
                                pr=pr, first=(nmm == 1),
                                last=(nmm == 3 * DK): e.matmul(
                            pr[:, :TT],
                            w[:, (dk * DK + dm) * 128:(dk * DK + dm + 1) * 128],
                            z[:, dk * TT:(dk + 1) * TT],
                            start=first, stop=last, skip_group_check=True),
                            sig=(sm if nmm == 3 * DK else None))
                kb.wait("s", sm, sm.n)
                kb.wait("s", sv, sv.n)
                kb.emit("s", lambda e, dm=dm, pr=pr: e.activation(
                    r1f[:, (dm % 2) * TT:(dm % 2 + 1) * TT], pr[:, :TT],
                    AF.Silu, bias=c["rb1c"][:, dm:dm + 1]), sig=ss)
                s_r1dm = ss.n
                kb.wait("v", ss, s_r1dm)
                kb.emit("v", lambda e, dm=dm: e.tensor_copy(
                    r1h[:, dm * TT:(dm + 1) * TT],
                    r1f[:, (dm % 2) * TT:(dm % 2 + 1) * TT]), sig=sv)
                kb.wait("v", sv, sv.n)
                kb.emit("v", lambda e, dm=dm: e.tensor_tensor(
                    r1l[:, dm * TT:(dm + 1) * TT],
                    r1f[:, (dm % 2) * TT:(dm % 2 + 1) * TT],
                    r1h[:, dm * TT:(dm + 1) * TT], ALU.subtract), sig=sv)
            v_r1 = sv.n
            st_["w1_rd"] = [sm.n] * NS
            st_["w2_rd"] = [sm.n] * NS

            kb.wait("t", sv, v_r1)
            kb.wait("t", sv, c["n_small_v"])
            lterm = ((c["rw2h"], r1h), (c["rw2h"], r1l), (c["rw2l"], r1h))
            nmm = 0
            for (wsrc, rsrc) in lterm:
                for dk in range(DK):
                    nmm += 1
                    kb.emit("t", lambda e, w=wsrc, r=rsrc, dk=dk,
                            first=(nmm == 1), last=(nmm == 3 * DK): e.matmul(
                        ps[4][0:8, :TT], w[:, dk * E:(dk + 1) * E],
                        r[:, dk * TT:(dk + 1) * TT],
                        start=first, stop=last, skip_group_check=True),
                        sig=(sm if nmm == 3 * DK else None))
            sm_lg = sm.n
            kb.wait("s", sm, sm_lg)
            kb.emit("s", lambda e: e.activation(
                lgs[:, :TT], ps[4][0:8, :TT], AF.Identity,
                bias=c["rb2c"][0:8, 0:1]), sig=ss)
            s_lgs = ss.n
            kb.wait("t", ss, s_lgs)
            for cc in range(NC4):
                kb.emit("t", lambda e, cc=cc: e.transpose(
                    ps[5][:, cc * 8:(cc + 1) * 8],
                    lgs[0:8, cc * 128:(cc + 1) * 128], c["identf"][0:8, 0:8]),
                    sig=(sm if cc == NC4 - 1 else None))
            sm_lgT = sm.n
            kb.wait("v", sm, sm_lgT)

            for cc in range(NC4):
                L, L2, L3 = lg8[:, :], l28[:, :], l38[:, :]
                kb.wait("v", sv, sv.n)
                kb.emit("v", lambda e, cc=cc: e.tensor_copy(
                    lg8[:, :], ps[5][:, cc * E:(cc + 1) * E]), sig=sv)
                m1 = m12[:, cc:cc + 1]
                m2 = m12[:, NC4 + cc:NC4 + cc + 1]
                m2n = m12[:, 2 * NC4 + cc:2 * NC4 + cc + 1]
                w1 = m12[:, 3 * NC4 + cc:3 * NC4 + cc + 1]
                w2 = m12[:, 4 * NC4 + cc:4 * NC4 + cc + 1]
                d12 = m12[:, 5 * NC4 + cc:5 * NC4 + cc + 1]
                w1r = m12[:, 6 * NC4 + cc:6 * NC4 + cc + 1]
                kb.wait("v", sv, sv.n)
                kb.emit("v", lambda e, L=L, m1=m1: e.reduce_max(
                    m1, L, axis=AX.X), sig=sv)
                kb.wait("v", sv, sv.n)
                kb.emit("v", lambda e, L=L, m1=m1: e.tensor_scalar(
                    m1x8[:, :], L, 0.0, m1, ALU.mult, ALU.add), sig=sv)
                kb.wait("v", sv, sv.n)
                kb.emit("v", lambda e, L=L, L2=L2: e.match_replace(
                    L2, m1x8[:, :], L, NEG_BIG), sig=sv)
                kb.wait("v", sv, sv.n)
                kb.emit("v", lambda e, L2=L2, m2=m2: e.reduce_max(
                    m2, L2, axis=AX.X), sig=sv)
                kb.emit("v", lambda e, L2=L2, m2n=m2n: e.tensor_reduce(
                    m2n, L2, axis=AX.X, op=ALU.max, negate=True), sig=sv)
                kb.wait("v", sv, sv.n)
                kb.emit("v", lambda e, L2=L2, m2=m2: e.tensor_scalar(
                    m2x8[:, :], L2, 0.0, m2, ALU.mult, ALU.add), sig=sv)
                kb.wait("v", sv, sv.n)
                kb.emit("v", lambda e, L2=L2, L3=L3: e.match_replace(
                    L3, m2x8[:, :], L2, NEG_BIG), sig=sv)
                v_l3 = sv.n
                kb.wait("s", sv, v_l3)
                kb.emit("s", lambda e, m1=m1, m2n=m2n, w1r=w1r: e.activation(
                    w1r, m1, AF.Sigmoid, bias=m2n), sig=ss)
                s_w1r = ss.n
                kb.wait("v", ss, s_w1r)
                kb.emit("v", lambda e, w1r=w1r, w1=w1: e.tensor_scalar(
                    w1, w1r, 0.5, None, ALU.mult), sig=sv)
                kb.wait("v", sv, sv.n)
                kb.emit("v", lambda e, w1=w1, w2=w2: e.tensor_scalar(
                    w2, w1, -1.0, 0.5, ALU.mult, ALU.add), sig=sv)
                kb.wait("v", sv, sv.n)
                kb.emit("v", lambda e, w1=w1, w2=w2, d12=d12: e.tensor_tensor(
                    d12, w1, w2, ALU.subtract), sig=sv)
                kb.emit("v", lambda e, L2=L2: e.tensor_scalar(
                    tp8[:, 0:E], L2, NEG_BIG, None, ALU.is_equal), sig=sv)
                kb.emit("v", lambda e, L3=L3: e.tensor_scalar(
                    tp8[:, E:2 * E], L3, NEG_BIG, None, ALU.is_equal), sig=sv)
                kb.wait("v", sv, sv.n)
                kb.emit("v", lambda e, d12=d12: e.tensor_scalar(
                    tp8[:, 2 * E:3 * E], tp8[:, 0:E], d12, None, ALU.mult),
                    sig=sv)
                kb.emit("v", lambda e, w2=w2: e.tensor_scalar(
                    tp8[:, 3 * E:4 * E], tp8[:, E:2 * E], w2, None, ALU.mult),
                    sig=sv)
                kb.wait("v", sv, sv.n)
                kb.emit("v", lambda e, cc=cc: e.tensor_tensor(
                    wtm[:, cc * E:(cc + 1) * E], tp8[:, 2 * E:3 * E],
                    tp8[:, 3 * E:4 * E], ALU.add), sig=sv)
            v_wtm = sv.n
            kb.wait("t", sv, v_wtm)
            for cc in range(NC4):
                kb.emit("t", lambda e, cc=cc: e.transpose(
                    ps[5][0:8, cc * 128:(cc + 1) * 128],
                    wtm[:, cc * E:(cc + 1) * E], c["identf"][:, :]),
                    sig=(sm if cc == NC4 - 1 else None))
            sm_wT = sm.n
            kb.wait("v", sm, sm_wT)
            kb.emit("v", lambda e: e.tensor_copy(
                wfm[0:8, :TT], ps[5][0:8, :TT]), sig=sv)
            st_["v_wfm"] = sv.n
            st_["sm_bnd"] = sm.n

        def expert_loop(ti, pending):
            par = ti % 2
            zh = zh2[:, par * DK * TT:(par + 1) * DK * TT]
            ndrain = 8 * (HC + 2 * HC // 2)
            per = (len(pending) + ndrain - 1) // max(1, ndrain)
            ptr = [0]

            def drain(n=1):
                for _ in range(n):
                    if ptr[0] < len(pending):
                        pending[ptr[0]]()
                        ptr[0] += 1

            for e_i in range(E):
                if first and ti == 0:
                    if e_i == 0:
                        emit_convert(0)
                    if e_i + 1 < E:
                        emit_convert(e_i + 1)
                kb.waittok("y", c["tok_exp"][e_i])
                kb.wait("t", sv, st_["v_wfm"])
                kb.wait("t", sv, st_["v_wbc"])
                kb.wait("t", sv, st_["v_accinit"])
                kb.emit("t", lambda e, ei=e_i: e.matmul(
                    ps[2][:, :TT], c["esel"][:, ei * 128:(ei + 1) * 128],
                    wfm[0:8, :TT],
                    start=True, stop=True, skip_group_check=True), sig=sm)
                sm_wbc = sm.n
                kb.wait("v", sm, sm_wbc)
                kb.wait("v", sv, sv.n)
                kb.emit("v", lambda e, ei=e_i: e.tensor_copy(
                    wbc[:, (ei % 2) * TT:(ei % 2 + 1) * TT], ps[2][:, :TT]),
                    sig=sv)
                st_["v_wbc"] = sv.n
                v_wbc = sv.n

                w1_loads = []
                silu_cnt = []
                dw = c["dw"]
                for hc in range(HC):
                    grp = hc // GHC
                    h2 = grp % NS
                    if hc % GHC == 0:
                        kb.wait("y", sm, st_["w1_rd"][h2])
                        w1_loads.append(dw.dma(
                            lambda e, ei=e_i, grp=grp, h2=h2: e.dma_start(
                                out=w1b[:, h2 * GW:(h2 + 1) * GW],
                                in_=io["we1_b"][ei, :,
                                                grp * GW:(grp + 1) * GW])))
                    ph = ps[4 + hc % 2]
                    kb.waittok("t", w1_loads[grp])
                    if hc < 2:
                        kb.wait("t", sv, st_["eo_rd"].get(id(ph), 0))
                    if hc >= 2:
                        kb.wait("t", ss, silu_cnt[hc - 2])
                    for dk in range(DK):
                        kb.emit("t", lambda e, hc=hc, dk=dk, ph=ph, h2=h2,
                                first=(dk == 0), last=(dk == DK - 1): e.matmul(
                            ph[:, :TT],
                            w1b[:, h2 * GW + ((hc % GHC) * DK + dk) * 128:
                                h2 * GW + ((hc % GHC) * DK + dk + 1) * 128],
                            zh[:, dk * TT:(dk + 1) * TT],
                            start=first, stop=last, skip_group_check=True),
                            sig=(sm if dk == DK - 1 else None))
                    if hc % GHC == GHC - 1:
                        st_["w1_rd"][h2] = sm.n
                    kb.wait("s", sm, sm.n)
                    kb.emit("s", lambda e, hc=hc, ei=e_i, ph=ph: e.activation(
                        h_sb[:, hc * TT:(hc + 1) * TT], ph[:, :TT], AF.Silu,
                        bias=c["be1c"][:, ei * HC + hc:ei * HC + hc + 1]),
                        sig=ss)
                    silu_cnt.append(ss.n)
                    drain(per)
                s_h = ss.n

                for half_i in range(2):
                    dms = list(range(half_i * (DK // 2),
                                     (half_i + 1) * (DK // 2)))
                    pe_map = {dm: ps[(4, 5, 6, 0)[dm % (DK // 2)]]
                              for dm in dms}
                    w2_loads = []
                    kb.wait("t", ss, s_h)
                    for hc in range(HC):
                        grp = hc // GHC
                        h2 = grp % NS
                        if hc % GHC == 0:
                            kb.wait("y", sm, st_["w2_rd"][h2])
                            w2_loads.append(c["dw"].dma(
                                lambda e, ei=e_i, grp=grp, h2=h2: e.dma_start(
                                    out=w2b[:, h2 * GW:(h2 + 1) * GW],
                                    in_=io["we2_b"][ei, :,
                                                    grp * GW:(grp + 1) * GW])))
                        kb.waittok("t", w2_loads[grp])
                        grp_last = (hc % GHC == GHC - 1)
                        if hc == 0:
                            for dm in dms:
                                kb.wait("t", sv,
                                        st_["eo_rd"].get(id(pe_map[dm]), 0))
                        for dm in dms:
                            is_last = (hc == HC - 1) and (e_i != 0)
                            kb.emit("t", lambda e, hc=hc, dm=dm, h2=h2,
                                    pe_=pe_map[dm], first=(hc == 0),
                                    last=is_last: e.matmul(
                                pe_[:, :TT],
                                w2b[:, h2 * GW + ((hc % GHC) * DK + dm) * 128:
                                    h2 * GW + ((hc % GHC) * DK + dm + 1) * 128],
                                h_sb[:, hc * TT:(hc + 1) * TT],
                                start=first, stop=last, skip_group_check=True),
                                sig=(sm if (is_last or grp_last) and
                                     dm == dms[-1] else None))
                        if grp_last:
                            st_["w2_rd"][h2] = sm.n
                        drain(per)
                    if e_i == 0:
                        for dm in dms:
                            kb.emit("t", lambda e, dm=dm, pe_=pe_map[dm]:
                                    e.matmul(
                                pe_[:, :TT],
                                c["be2r"][0:8, dm * 128:(dm + 1) * 128],
                                wfm[0:8, :TT],
                                start=False, stop=True, skip_group_check=True),
                                sig=(sm if dm == dms[-1] else None))
                    sm_eo = sm.n
                    kb.wait("v", sm, sm_eo)
                    kb.wait("v", sv, v_wbc)
                    for dm in dms:
                        kb.wait("v", sv, sv.n)
                        kb.wait("v", sg, st_["g_eot"][dm % 2])
                        kb.emit("v", lambda e, dm=dm, ei=e_i,
                                pe_=pe_map[dm]: e.tensor_tensor(
                            eot[:, (dm % 2) * TT:(dm % 2 + 1) * TT],
                            pe_[:, :TT],
                            wbc[:, (ei % 2) * TT:(ei % 2 + 1) * TT],
                            ALU.mult), sig=sv)
                        v_eot = sv.n
                        st_["eo_rd"][id(pe_map[dm])] = v_eot
                        kb.wait("g", sv, v_eot)
                        kb.wait("g", sg, st_["g_acc"][dm])
                        kb.emit("g", lambda e, dm=dm: e.tensor_tensor(
                            acc[:, dm * TT:(dm + 1) * TT],
                            acc[:, dm * TT:(dm + 1) * TT],
                            eot[:, (dm % 2) * TT:(dm % 2 + 1) * TT], ALU.add),
                            sig=sg)
                        st_["g_acc"][dm] = sg.n
                        st_["g_eot"][dm % 2] = sg.n
            drain(10 ** 6)
            kb.wait("y", sg, max(st_["g_acc"]))
            st_["acc_dma"] = dy.dma(lambda e, t=slice(ti * TT, (ti + 1) * TT):
                                    e.dma_start(
                out=dst_r[:, :, t],
                in_=acc[:].rearrange("p (dk t) -> p dk t", dk=DK)))

        for t in make_prep(0):
            t()
        for ti in range(ntiles):
            boundary(ti)
            pending = make_prep(ti + 1) if ti + 1 < ntiles else []
            expert_loop(ti, pending)
        kb.waittok("y", dy.token())
    return


def build_qkv(kb, io, c, q_pair):
    sm, sv, ss, sg, dy, dg = c["sm"], c["sv"], c["ss"], c["sg"], c["dy"], c["dg"]
    ps, onesrow, onescol = c["ps"], c["onesrow"], c["onescol"]
    HD = NH * DH
    NHP = HD // 128
    T2 = min(512, SEQ)
    NT2 = SEQ // T2
    QSC = float(DH ** -0.5)

    with contextlib.ExitStack() as stk:
        sb = lambda n, s, d: stk.enter_context(kb.nc.sbuf_tensor(n, s, d))
        yt = sb("q_yt", [128, DK * T2], F32)
        sq = sb("q_sq", [128, 2 * T2], F32)
        z2 = sb("q_z2", [128, DK * T2], BF16)
        hb = sb("q_hb", [128, DK * T2], BF16)
        zf = sb("q_zf", [128, T2], F32)
        wqb = sb("q_wq", [128, DK * NHP * 128], BF16)
        wkb = sb("q_wk", [128, DK * NHP * 128], BF16)
        wvb = sb("q_wv", [128, DK * NHP * 128], BF16)
        kst = sb("q_kst", [128, NHP * T2], BF16)
        vst = sb("q_vst", [128, NHP * T2], BF16)
        rows = sb("q_rows", [1, 8 * T2], F32)

        wq_ld = dg.dma(lambda e: e.dma_start(out=wqb[:], in_=io["wq_t"][:]))
        wk_ld = dg.dma(lambda e: e.dma_start(out=wkb[:], in_=io["wk_t"][:]))
        wv_ld = dg.dma(lambda e: e.dma_start(out=wvb[:], in_=io["wv_t"][:]))

        row_sl = lambda i: rows[0:1, i * T2:(i + 1) * T2]
        R_MEAN, R_MS, R_SQM, R_VAR, R_SD, R_RSTD, R_B, R_R2 = range(8)
        src_r = io["y1_d"].rearrange("(dk p) s -> p dk s", p=128)

        st_ = {"y_tok": None, "v_z2": 0, "s_sq": 0, "sm_bnd": 0,
               "_sm_sq2": 0, "_v_rows": 0, "_sm_bc": 0,
               "kv_tok": None, "v_h": 0}

        def make_prep(ti):
            tasks = []
            tsl = slice(ti * T2, (ti + 1) * T2)

            def t_yload():
                kb.wait("y", sv, st_["v_z2"])
                kb.wait("y", ss, st_["s_sq"])
                st_["y_tok"] = dy.dma(lambda e, t=tsl: e.dma_start(
                    out=yt[:].rearrange("p (dk t) -> p dk t", dk=DK),
                    in_=src_r[:, :, t]))
            tasks.append(t_yload)

            def t_sum():
                kb.waittok("t", st_["y_tok"])
                kb.wait("t", sv, st_["v_z2"])
                for dk in range(DK):
                    kb.emit("t", lambda e, dk=dk, last=(dk == DK - 1):
                            e.matmul(
                        ps[1][0:1, :T2], onescol[:, 0:1],
                        yt[:, dk * T2:(dk + 1) * T2],
                        start=(dk == 0), stop=last, skip_group_check=True),
                        sig=(sm if dk == DK - 1 else None))
                sm_sum = sm.n
                kb.wait("v", sm, sm_sum)
                kb.wait("v", sm, st_["sm_bnd"])
                kb.emit("v", lambda e: e.tensor_scalar(
                    row_sl(R_MEAN), ps[1][0:1, :T2], 1.0 / D, None, ALU.mult),
                    sig=sv)
            tasks.append(t_sum)

            def t_sq(dk):
                kb.waittok("s", st_["y_tok"])
                if dk >= 2:
                    kb.wait("s", sm, st_["_sm_sq2"])
                kb.emit("s", lambda e, dk=dk: e.activation(
                    sq[:, (dk % 2) * T2:(dk % 2 + 1) * T2],
                    yt[:, dk * T2:(dk + 1) * T2], AF.Square), sig=ss)
                st_["s_sq"] = ss.n
                kb.wait("t", ss, ss.n)
                if dk == 0:
                    kb.wait("t", sv, sv.n)
                kb.emit("t", lambda e, dk=dk, last=(dk == DK - 1): e.matmul(
                    ps[1][0:1, :T2], onescol[:, 0:1],
                    sq[:, (dk % 2) * T2:(dk % 2 + 1) * T2],
                    start=(dk == 0), stop=last, skip_group_check=True), sig=sm)
                st_["_sm_sq2"] = sm.n
            for dk in range(DK):
                tasks.append(lambda dk=dk: t_sq(dk))

            def t_chain():
                kb.wait("v", sm, st_["_sm_sq2"])
                kb.emit("v", lambda e: e.tensor_scalar(
                    row_sl(R_MS), ps[1][0:1, :T2], 1.0 / D, None, ALU.mult),
                    sig=sv)
                v_ms = sv.n
                kb.wait("s", sv, v_ms)
                kb.emit("s", lambda e: e.activation(
                    row_sl(R_SQM), row_sl(R_MEAN), AF.Square), sig=ss)
                s_sqm = ss.n
                kb.wait("v", ss, s_sqm)
                kb.emit("v", lambda e: e.tensor_tensor(
                    row_sl(R_VAR), row_sl(R_MS), row_sl(R_SQM), ALU.subtract),
                    sig=sv)
                v_var = sv.n
                kb.wait("s", sv, v_var)
                kb.wait("s", c["sg"], c["sg_ones"])
                kb.emit("s", lambda e: e.activation(
                    row_sl(R_SD), row_sl(R_VAR), AF.Sqrt,
                    bias=c["cst"][0:1, 0:1]), sig=ss)
                s_sd = ss.n
                kb.wait("v", ss, s_sd)
                kb.emit("v", lambda e: e.reciprocal(
                    row_sl(R_RSTD), row_sl(R_SD)), sig=sv)
                kb.wait("v", sv, sv.n)
                kb.emit("v", lambda e: e.tensor_tensor(
                    row_sl(R_B), row_sl(R_MEAN), row_sl(R_RSTD), ALU.mult),
                    sig=sv)
                st_["_v_rows"] = sv.n
            tasks.append(t_chain)

            def t_bc():
                kb.wait("t", sv, st_["_v_rows"])
                kb.emit("t", lambda e: e.matmul(
                    ps[2][:, :T2], onesrow[:, :], row_sl(R_RSTD),
                    start=True, stop=True, skip_group_check=True), sig=sm)
                kb.emit("t", lambda e: e.matmul(
                    ps[3][:, :T2], onesrow[:, :], row_sl(R_B),
                    start=True, stop=True, skip_group_check=True), sig=sm)
                st_["_sm_bc"] = sm.n
            tasks.append(t_bc)

            def t_z(dk):
                dsl = slice(dk * T2, (dk + 1) * T2)
                kb.wait("v", sm, st_["_sm_bc"])
                kb.wait("v", sm, st_["sm_bnd"])
                kb.wait("v", sv, sv.n)
                kb.emit("v", lambda e, dsl=dsl: e.tensor_tensor(
                    zf[:], yt[:, dsl], ps[2][:, :T2], ALU.mult), sig=sv)
                kb.wait("v", sv, sv.n)
                kb.emit("v", lambda e, dsl=dsl: e.tensor_tensor(
                    z2[:, dsl], zf[:], ps[3][:, :T2], ALU.subtract), sig=sv)
                st_["v_z2"] = sv.n
            for dk in range(DK):
                tasks.append(lambda dk=dk: t_z(dk))
            return tasks

        def boundary_and_proj(ti, pending):
            per = (len(pending) + 24) // 25
            ptr = [0]

            def drain(n=1):
                for _ in range(n):
                    if ptr[0] < len(pending):
                        pending[ptr[0]]()
                        ptr[0] += 1

            kb.wait("t", sv, st_["v_z2"])
            for dk in range(DK):
                kb.wait("s", sv, st_["v_z2"])
                if dk >= 2:
                    kb.wait("s", sm, sm.n)
                kb.emit("s", lambda e, dk=dk: e.activation(
                    sq[:, (dk % 2) * T2:(dk % 2 + 1) * T2],
                    z2[:, dk * T2:(dk + 1) * T2], AF.Square), sig=ss)
                st_["s_sq"] = ss.n
                kb.wait("t", ss, ss.n)
                if dk == 0:
                    kb.wait("t", sv, sv.n)
                kb.emit("t", lambda e, dk=dk, last=(dk == DK - 1): e.matmul(
                    ps[0][0:1, :T2], onescol[:, 0:1],
                    sq[:, (dk % 2) * T2:(dk % 2 + 1) * T2],
                    start=(dk == 0), stop=last, skip_group_check=True), sig=sm)
            sm_ss2 = sm.n
            kb.wait("v", sm, sm_ss2)
            kb.emit("v", lambda e: e.tensor_scalar(
                row_sl(R_MS), ps[0][0:1, :T2], 1.0 / D, None, ALU.mult), sig=sv)
            v_ms2 = sv.n
            kb.wait("s", sv, v_ms2)
            kb.emit("s", lambda e: e.activation(
                row_sl(R_SD), row_sl(R_MS), AF.Sqrt,
                bias=c["cst"][0:1, 1:2]), sig=ss)
            s_sd2 = ss.n
            kb.wait("v", ss, s_sd2)
            kb.emit("v", lambda e: e.reciprocal(row_sl(R_R2), row_sl(R_SD)),
                    sig=sv)
            v_r2 = sv.n
            kb.wait("t", sv, v_r2)
            kb.emit("t", lambda e: e.matmul(
                ps[0][:, :T2], onesrow[:, :], row_sl(R_R2),
                start=True, stop=True, skip_group_check=True), sig=sm)
            sm_r2bc = sm.n
            kb.wait("v", sm, sm_r2bc)
            for dk in range(DK):
                dsl = slice(dk * T2, (dk + 1) * T2)
                kb.wait("v", sv, sv.n)
                kb.emit("v", lambda e, dsl=dsl, dk=dk: e.scalar_tensor_tensor(
                    hb[:, dsl], z2[:, dsl], c["scalec"][:, dk:dk + 1],
                    ps[0][:, :T2], ALU.mult, ALU.mult), sig=sv)
            v_h = sv.n

            kb.waittok("t", wq_ld)
            kb.waittok("t", wk_ld)
            kb.waittok("t", wv_ld)
            kb.wait("t", sv, v_h)
            own_tile = (ti * T2) < OWN
            prev_ev = {4: ("sv", 0), 5: ("sv", 0)}
            for proj, wb in (("q", wqb), ("k", wkb), ("v", wvb)):
                if proj == "q" and not own_tile:
                    continue
                for cc in range(NHP):
                    bk = 4 + cc % 2
                    pq = ps[bk]
                    pk, pv_ = prev_ev[bk]
                    kb.wait("t", sv if pk == "sv" else ss, pv_)
                    for dk in range(DK):
                        kb.emit("t", lambda e, wb=wb, cc=cc, dk=dk, pq=pq,
                                last=(dk == DK - 1): e.matmul(
                            pq[:, :T2],
                            wb[:, (dk * NHP + cc) * 128:
                               (dk * NHP + cc + 1) * 128],
                            hb[:, dk * T2:(dk + 1) * T2],
                            start=(dk == 0), stop=last,
                            skip_group_check=True),
                            sig=(sm if dk == DK - 1 else None))
                    sm_p = sm.n
                    if proj == "q":
                        kb.wait("s", sm, sm_p)
                        qw_ = min(T2, OWN - ti * T2)
                        kb.emit("s", lambda e, cc=cc, t0=ti * T2, pq=pq,
                                qw_=qw_: e.activation(
                            q_pair[:, cc * OWN + t0:cc * OWN + t0 + qw_],
                            pq[:, :qw_], AF.Identity, scale=QSC), sig=ss)
                        prev_ev[bk] = ("ss", ss.n)
                    else:
                        dst = kst if proj == "k" else vst
                        kb.wait("v", sm, sm_p)
                        kb.waittok("v", st_["kv_tok"])
                        kb.emit("v", lambda e, cc=cc, dst=dst, pq=pq:
                                e.tensor_copy(
                            dst[:, cc * T2:(cc + 1) * T2], pq[:, :T2]),
                            sig=sv)
                        prev_ev[bk] = ("sv", sv.n)
                    drain(per)
            v_kv = sv.n
            kb.wait("y", sv, v_kv)
            kb.wait("y", ss, ss.n)
            tsl = slice(ti * T2, (ti + 1) * T2)
            tk1 = dy.dma(lambda e, t=tsl: e.dma_start(
                out=io["kf_d"].rearrange("(cc p) s -> p cc s", p=128)[:, :, t],
                in_=kst[:].rearrange("p (cc t) -> p cc t", cc=NHP)))
            tk2 = dy.dma(lambda e, t=tsl: e.dma_start(
                out=io["vf_d"].rearrange("(cc p) s -> p cc s", p=128)[:, :, t],
                in_=vst[:].rearrange("p (cc t) -> p cc t", cc=NHP)))
            st_["kv_tok"] = [tk1, tk2]
            st_["sm_bnd"] = sm.n
            drain(len(pending))

        for t in make_prep(0):
            t()
        for ti in range(NT2):
            pending = make_prep(ti + 1) if ti + 1 < NT2 else []
            boundary_and_proj(ti, pending)
        kb.waittok("y", dy.token())
    c["s_q_done"] = ss.n


def build_attention(kb, io, c, q_pair):
    sm, sv, ss, sg, dy, dg = c["sm"], c["sv"], c["ss"], c["sg"], c["dy"], c["dg"]
    ps, onesrow = c["ps"], c["onesrow"]
    ptb = c["ptb"]
    NKC = SEQ // 128
    NCK = SEQ // 512
    NQT = OWN // 128
    QB = min(512, OWN)
    NQB = OWN // QB

    with contextlib.ExitStack() as stk:
        sb = lambda n, s, d: stk.enter_context(kb.nc.sbuf_tensor(n, s, d))
        k_ext = [sb(f"a_ke{i}", [65, SEQ], BF16) for i in range(2)]
        kbuf = [sb(f"a_kb{i}", [128, SEQ], BF16) for i in range(2)]
        v_ext = [sb(f"a_ve{i}", [128, NKC * 65], BF16) for i in range(2)]
        q_ext = [sb(f"a_qe{i}", [65, OWN], BF16) for i in range(2)]
        mrowT = sb("a_mr", [65, OWN], BF16)
        mrowb = [mrowT[0:1, :], mrowT[64:65, :]]
        p_tm = sb("a_ptm", [128, 4 * QB], BF16)
        ksqb = sb("a_ksq", [128, 2 * 512], BF16)
        pmax = sb("a_pmx", [65, NCK], F32)
        maxk = sb("a_mxk", [65, 2], F32)
        ones2b = sb("a_o2b", [128, 65], BF16)
        rrow = sb("a_rrow", [1, 2 * QB], F32)
        rrep = sb("a_rrep", [64, 2 * QB], F32)
        o_ev = sb("a_oev", [64, 2 * QB], BF16)

        kf = io["kf_d"]
        vf = io["vf_d"]
        od = io["o_d"]

        st_ = {
            "kbuf_tok": None, "kbuf_rd": [0, 0], "kbuf_ss": [0, 0],
            "kext_rd": [0, 0], "qext_rd": [0, 0], "vext_rd": [0, 0],
            "sqb_rd": [0, 0], "p0_rd": 0,
            "v_kmax": 0, "mrow_s": 0, "mrow_rd": 0,
            "ones2_tok": None,
            "ptb_rd": 0,
            "a_bank_rd": [0] * 7,
            "exp_cnt": {}, "pv_sm": {},
            "po_rd": 0, "o_tok": [None, None],
        }

        kb.emit("g", lambda e: e.memset(ones2b[:, :], 0.0), sig=sg)
        kb.wait("g", sg, sg.n)
        kb.emit("g", lambda e: e.memset(ones2b[0:64, 0:1], 1.0), sig=sg)
        kb.emit("g", lambda e: e.memset(ones2b[64:128, 64:65], 1.0), sig=sg)
        st_["ones2_tok"] = (sg, sg.n)

        KC = min(512, SEQ)
        QC = min(512, OWN)
        NQC = OWN // QC

        def make_pair_tasks(j):
            par = j % 2
            kbuf_ = kbuf[par]
            tA, tB = [], []

            def t_k2(j=j, par=par):
                kb.wait("g", sm, st_["kbuf_rd"][par])
                kb.wait("g", ss, st_["kbuf_ss"][par])
                st_["kbuf_tok"] = dg.dma(lambda e, j=j: e.dma_start(
                    out=kbuf_[:, :], in_=kf[j * 128:(j + 1) * 128, :]))
            tA.append(t_k2)

            for ck in range(NCK):
                def t_ksq(ck=ck, par=par, kbuf_=kbuf_):
                    half = ck % 2
                    kb.waittok("s", st_["kbuf_tok"])
                    kb.wait("s", sm, st_["sqb_rd"][half])
                    kb.emit("s", lambda e, ck=ck, half=half: e.activation(
                        ksqb[:, half * 512:half * 512 + KC],
                        kbuf_[:, ck * KC:(ck + 1) * KC], AF.Square), sig=ss)
                    st_["kbuf_ss"][par] = ss.n
                    kb.wait("t", ss, ss.n)
                    kb.wait("t", sv, st_["p0_rd"])
                    kb.waittok("t", [st_["ones2_tok"]])
                    kb.emit("t", lambda e, half=half: e.matmul(
                        ps[0][0:65, :KC], ones2b[:, :],
                        ksqb[:, half * 512:half * 512 + KC],
                        start=True, stop=True, skip_group_check=True), sig=sm)
                    st_["sqb_rd"][half] = sm.n
                    kb.wait("v", sm, sm.n)
                    kb.emit("v", lambda e, ck=ck: e.reduce_max(
                        pmax[:, ck:ck + 1], ps[0][0:65, :KC],
                        axis=AX.X), sig=sv)
                    st_["p0_rd"] = sv.n
                tA.append(t_ksq)

            def t_kmax(par=par):
                kb.wait("v", ss, st_["mrow_s"])
                kb.wait("v", sv, sv.n)
                kb.emit("v", lambda e, par=par: e.tensor_reduce(
                    maxk[:, par:par + 1], pmax[:, 0:NCK], axis=AX.X,
                    op=ALU.max), sig=sv)
                st_["v_kmax"] = sv.n
            tA.append(t_kmax)

            for qc in range(NQC):
                def t_qsq(qc=qc, par=par, j=j):
                    half = qc % 2
                    kb.wait("s", sm, st_["sqb_rd"][half])
                    kb.emit("s", lambda e, qc=qc, half=half, j=j:
                            e.activation(
                        ksqb[:, half * 512:half * 512 + QC],
                        q_pair[:, j * OWN + qc * QC:j * OWN + (qc + 1) * QC],
                        AF.Square), sig=ss)
                    kb.wait("t", ss, ss.n)
                    kb.wait("t", ss, st_["mrow_s"])
                    kb.waittok("t", [st_["ones2_tok"]])
                    kb.emit("t", lambda e, half=half: e.matmul(
                        ps[1][0:65, :QC], ones2b[:, :],
                        ksqb[:, half * 512:half * 512 + QC],
                        start=True, stop=True, skip_group_check=True), sig=sm)
                    st_["sqb_rd"][half] = sm.n
                    kb.wait("s", sm, sm.n)
                    kb.wait("s", sv, st_["v_kmax"])
                    kb.wait("s", sv, st_["mrow_rd"])
                    kb.emit("s", lambda e, qc=qc, par=par: e.activation(
                        mrowT[:, qc * QC:(qc + 1) * QC],
                        ps[1][0:65, :QC], AF.Sqrt,
                        scale=maxk[:, par:par + 1]), sig=ss)
                    st_["mrow_s"] = ss.n
                tB.append(t_qsq)
            return tA, tB

        def emit_bpv(h, pending):
            hi = h % 2
            j = h // 2
            ke = k_ext[hi]
            ve = v_ext[hi]
            qe = q_ext[hi]
            niter = NQB * (NKC + 2)
            per = (len(pending) + niter - 1) // max(1, niter)
            ti_ = [0]
            norm_q = []

            def drain(n):
                for _ in range(n):
                    if ti_[0] < len(pending):
                        pending[ti_[0]]()
                        ti_[0] += 1

            kb.wait("g", sm, st_["kext_rd"][hi])
            kext_tok = dg.dma(lambda e, h=h, hi=hi: e.dma_start(
                out=k_ext[hi][0:64, :], in_=kf[h * 64:(h + 1) * 64, :]))
            kb.emit("g", lambda e, hi=hi: e.memset(
                k_ext[hi][64:65, :], -1.0), sig=sg)
            kext_tok = [kext_tok, (sg, sg.n)]
            vband = kbuf[j % 2][0:64, :]
            kb.wait("g", sm, st_["kbuf_rd"][j % 2])
            kb.wait("g", ss, st_["kbuf_ss"][j % 2])
            vband_tok = dg.dma(lambda e, h=h, vband=vband: e.dma_start(
                out=vband[:, :], in_=vf[h * 64:(h + 1) * 64, :]))
            kb.emit("g", lambda e, hi=hi: e.memset(
                v_ext[hi][:].rearrange("p (a b) -> p a b", b=65)[:, :, 64:65],
                1.0), sig=sg)
            vext_ones = (sg, sg.n)
            kb.wait("t", sm, st_["vext_rd"][hi])
            for kc0 in range(0, NKC, 8):
                nb = min(8, NKC - kc0)
                kb.waittok("t", vband_tok)
                kb.wait("t", sv, st_["ptb_rd"])
                for b in range(nb):
                    kb.emit("t", lambda e, kc=kc0 + b, b=b, vband=vband:
                            e.transpose(
                        ptb[:, b * 64:(b + 1) * 64],
                        vband[:, kc * 128:(kc + 1) * 128],
                        c["identb"][0:64, 0:64]),
                        sig=(sm if b == nb - 1 else None))
                sm_tr = sm.n
                kb.wait("v", sm, sm_tr)
                kb.waittok("v", [vext_ones])
                for b in range(nb):
                    kb.emit("v", lambda e, hi=hi, kc=kc0 + b, b=b:
                            e.tensor_copy(
                        v_ext[hi][:, kc * 65:kc * 65 + 64],
                        ptb[:, b * 64:(b + 1) * 64]),
                        sig=(sv if b == nb - 1 else None))
                st_["ptb_rd"] = sv.n
            st_["kbuf_rd"][j % 2] = sm.n
            v_vext = sv.n
            kb.wait("v", sm, st_["qext_rd"][hi])
            kb.wait("v", ss, st_["mrow_s"])
            kb.emit("v", lambda e, hi=hi, j=j: e.tensor_copy(
                q_ext[hi][0:64, :],
                q_pair[hi * 64:hi * 64 + 64, j * OWN:(j + 1) * OWN]), sig=sv)
            kb.emit("v", lambda e, hi=hi: e.tensor_copy(
                q_ext[hi][64:65, :], mrowb[hi][0:1, :]), sig=sv)
            v_qx = sv.n
            st_["mrow_rd"] = v_qx

            kb.waittok("t", kext_tok)
            kb.wait("t", sv, v_qx)
            kb.wait("t", sv, v_vext)
            kb.wait("t", ss, ss.n)
            kb.wait("s", sm, sm.n)
            st_["exp_cnt"] = {}
            st_["pv_sm"] = {}
            for qb in range(NQB):
                q0 = qb * QB
                po = ps[4]
                for step in range(NKC + 2):
                    if step < NKC:
                        kc = step
                        pb = ps[2 + kc % 2]
                        if kc >= 2:
                            kb.wait("t", ss, st_["exp_cnt"][(qb, kc - 2)])
                        elif qb > 0:
                            kb.wait("t", ss,
                                    st_["exp_cnt"][(qb - 1, NKC - 2 + kc)])
                        kb.emit("t", lambda e, kc=kc, q0=q0, pb=pb: e.matmul(
                            pb[:, :QB], ke[:, kc * 128:(kc + 1) * 128],
                            qe[:, q0:q0 + QB],
                            start=True, stop=True, skip_group_check=True),
                            sig=sm)
                        sm_b = sm.n
                        kb.wait("s", sm, sm_b)
                        pkey = (qb, kc - 4) if kc >= 4 else (
                            (qb - 1, NKC - 4 + kc) if qb > 0 else None)
                        if pkey in st_["pv_sm"]:
                            kb.wait("s", sm, st_["pv_sm"][pkey])
                        kb.emit("s", lambda e, kc=kc, pb=pb: e.activation(
                            p_tm[:, (kc % 4) * QB:(kc % 4 + 1) * QB],
                            pb[:, :QB], AF.Exp), sig=ss)
                        st_["exp_cnt"][(qb, kc)] = ss.n
                    if step >= 2:
                        kc = step - 2
                        kb.wait("t", ss, st_["exp_cnt"][(qb, kc)])
                        if kc == 0:
                            kb.wait("t", sv, st_["po_rd"])
                        is_last = (kc == NKC - 1)
                        kb.emit("t", lambda e, kc=kc, po=po,
                                first=(kc == 0), last=is_last: e.matmul(
                            po[0:65, :QB], ve[:, kc * 65:(kc + 1) * 65],
                            p_tm[:, (kc % 4) * QB:(kc % 4 + 1) * QB],
                            start=first, stop=last, skip_group_check=True),
                            sig=sm)
                        st_["pv_sm"][(qb, kc)] = sm.n
                    drain(per)
                sm_pv = sm.n

                def normalize(qb=qb, po=po, sm_pv=sm_pv, h=h):
                    par = qb % 2
                    kb.wait("v", sm, sm_pv)
                    kb.waittok("v", st_["o_tok"][par])
                    kb.emit("v", lambda e: e.reciprocal(
                        rrow[0:1, par * QB:par * QB + QB], po[64:65, :QB]),
                        sig=sv)
                    v_rr = sv.n
                    kb.wait("t", sv, v_rr)
                    kb.wait("t", sv, st_["a_bank_rd"][6])
                    kb.emit("t", lambda e: e.matmul(
                        ps[6][0:64, :QB], onesrow[0:1, 0:64],
                        rrow[0:1, par * QB:par * QB + QB],
                        start=True, stop=True, skip_group_check=True), sig=sm)
                    sm_rr = sm.n
                    kb.wait("v", sm, sm_rr)
                    kb.emit("v", lambda e: e.tensor_copy(
                        rrep[:, par * QB:par * QB + QB], ps[6][0:64, :QB]),
                        sig=sv)
                    st_["a_bank_rd"][6] = sv.n
                    kb.wait("v", sv, sv.n)
                    kb.emit("v", lambda e: e.tensor_tensor(
                        o_ev[:, par * QB:par * QB + QB], po[0:64, :QB],
                        rrep[:, par * QB:par * QB + QB], ALU.mult), sig=sv)
                    v_oev = sv.n
                    st_["po_rd"] = v_oev
                    kb.wait("y", sv, v_oev)
                    st_["o_tok"][par] = dy.dma(
                        lambda e, h=h, q0=qb * QB, par=par: e.dma_start(
                            out=od[h * 64:(h + 1) * 64, q0:q0 + QB],
                            in_=o_ev[:, par * QB:par * QB + QB]))
                normalize()
            drain(len(pending))
            st_["kext_rd"][hi] = sm.n
            st_["qext_rd"][hi] = sm.n
            st_["vext_rd"][hi] = sm.n

        tA0, tB0 = make_pair_tasks(0)
        for t in tA0 + tB0:
            t()
        for j in range(NH // 2):
            if j + 1 < NH // 2:
                pA, pB = make_pair_tasks(j + 1)
            else:
                pA, pB = [], []
            emit_bpv(2 * j, pA)
            emit_bpv(2 * j + 1, pB)
        kb.waittok("y", dy.token())


def build_outproj(kb, io, c):
    sm, sv, ss, sg, dy, dg = c["sm"], c["sv"], c["ss"], c["sg"], c["dy"], c["dg"]
    ps = c["ps"]
    HD = NH * DH
    NHP = HD // 128
    T2 = min(1024, OWN)
    NT2 = OWN // T2
    with contextlib.ExitStack() as stk:
        sb = lambda n, s, d: stk.enter_context(kb.nc.sbuf_tensor(n, s, d))
        ot = sb("o_ot", [128, NHP * T2], BF16)
        yt = sb("o_yt", [128, DK * T2], F32)
        tv = sb("o_tv", [128, 2 * T2], F32)
        y2 = sb("o_y2", [128, DK * T2], F32)
        wob = sb("o_wo", [128, NHP * DK * 128], BF16)

        wo_ld = dg.dma(lambda e: e.dma_start(out=wob[:], in_=io["wo_t"][:]))
        y1_r = io["y1_d"].rearrange("(dk p) s -> p dk s", p=128)
        y2_r = io["y2_d"].rearrange("(dk p) s -> p dk s", p=128)
        od_r = io["o_d"].rearrange("(cc p) s -> p cc s", p=128)
        prev_v = 0
        prev_store = None
        for ti in range(NT2):
            tsl = slice(ti * T2, (ti + 1) * T2)
            kb.wait("y", sv, prev_v)
            o_load = dy.dma(lambda e, t=tsl: e.dma_start(
                out=ot[:].rearrange("p (cc t) -> p cc t", cc=NHP),
                in_=od_r[:, :, t]))
            y_load = dy.dma(lambda e, t=tsl: e.dma_start(
                out=yt[:].rearrange("p (dk t) -> p dk t", dk=DK),
                in_=y1_r[:, :, t]))
            kb.waittok("t", o_load)
            kb.waittok("t", wo_ld)
            HW2 = min(512, T2)
            NHF2 = T2 // HW2
            for dm in range(DK):
                pqs = (ps[4], ps[5]) if dm % 2 == 0 else (ps[0], ps[1])
                kb.wait("t", ss, ss.n)
                for half in range(NHF2):
                    for cc in range(NHP):
                        kb.emit("t", lambda e, cc=cc, dm=dm, half=half,
                                pq=pqs[half]: e.matmul(
                            pq[:, :HW2],
                            wob[:, (cc * DK + dm) * 128:(cc * DK + dm + 1) * 128],
                            ot[:, cc * T2 + half * HW2:
                               cc * T2 + (half + 1) * HW2],
                            start=(cc == 0), stop=(cc == NHP - 1),
                            skip_group_check=True),
                            sig=(sm if cc == NHP - 1 and half == NHF2 - 1
                                 else None))
                sm_o = sm.n
                kb.wait("s", sm, sm_o)
                kb.wait("s", sv, sv.n)
                for half in range(NHF2):
                    kb.emit("s", lambda e, dm=dm, half=half, pq=pqs[half]:
                            e.activation(
                        tv[:, (dm % 2) * T2 + half * HW2:
                           (dm % 2) * T2 + (half + 1) * HW2], pq[:, :HW2],
                        AF.Identity, bias=c["boutc"][:, dm:dm + 1]), sig=ss)
                s_tv = ss.n
                kb.waittok("v", y_load)
                kb.wait("v", ss, s_tv)
                kb.wait("v", sv, sv.n)
                kb.emit("v", lambda e, dm=dm: e.tensor_tensor(
                    y2[:, dm * T2:(dm + 1) * T2],
                    tv[:, (dm % 2) * T2:(dm % 2 + 1) * T2],
                    yt[:, dm * T2:(dm + 1) * T2], ALU.add), sig=sv)
            v_y2 = sv.n
            prev_v = v_y2
            kb.wait("y", sv, v_y2)
            kb.waittok("y", prev_store)
            prev_store = dy.dma(lambda e, t=tsl: e.dma_start(
                out=y2_r[:, :, t],
                in_=y2[:].rearrange("p (dk t) -> p dk t", dk=DK)))
        kb.waittok("y", dy.token())


def build_program():
    kb = KB()
    io = declare_io(kb)
    c = declare_common(kb, io)
    build_prepass(kb, io, c)
    kb.barrier()
    build_ffn(kb, io, c, src=io["x_fm"], dst=io["y1_d"], ntiles=SEQ // TT,
              first=True)
    kb.barrier()
    with kb.nc.sbuf_tensor("q_pair", [128, (NH // 2) * OWN], BF16) as q_pair:
        build_qkv(kb, io, c, q_pair)
        kb.barrier()
        build_attention(kb, io, c, q_pair)
        kb.barrier()
    build_outproj(kb, io, c)
    kb.barrier()
    build_ffn(kb, io, c, src=io["y2_d"], dst=io["out_fm"], ntiles=OWN // TT)
    kb.barrier()
    kb.finalize()
    kb.es.close()
    return kb.nc


def kernel(**inputs):
    in_maps = host_prepare(inputs)
    nc = build_program()
    res = run_bass_kernel_spmd(nc, in_maps, core_ids=list(range(8)))
    return host_assemble(res.results)
